# version 12
# baseline (speedup 1.0000x reference)
import math
import numpy as np
import ml_dtypes
import concourse.bass as bass
import concourse.mybir as mybir
from concourse.bass_utils import run_bass_kernel_spmd
from contextlib import ExitStack

F32 = mybir.dt.float32
BF16 = mybir.dt.bfloat16
AF = mybir.ActivationFunctionType
ALU = mybir.AluOpType
AX = mybir.AxisListType

S = 2048
D = 1024
NH = 16
HD = 32
NSUB = 32
NCHUNK = 11
FF = 2048
NEXP = 8
LAMBDA_INIT = 0.8 - 0.6 * math.exp(-0.3 * 12)
SCALING = HD ** -0.5
EPS = 1e-5


class _It:
    __slots__ = ("eng", "fn", "deps", "cnt", "tag")

    def __init__(self, eng, fn, deps=(), tag=""):
        self.eng, self.fn, self.deps, self.tag = eng, fn, list(deps), tag
        self.cnt = None


def _run_items(nc, C, sems, lists):
    semmap = dict(pe=sems["sp"], act=sems["sa"], ve=sems["sv"], gp=sems["sg"],
                  sy=sems["sd"])
    ctrmap = dict(pe="sp", act="sa", ve="sv", gp="sg", sy="sd")
    step = dict(pe=1, act=1, ve=1, gp=16, sy=16)
    for eng, items in lists.items():
        base = C.get(ctrmap[eng])
        for i, it in enumerate(items):
            it.cnt = base + step[eng] * (i + 1)
        C.add(ctrmap[eng], step[eng] * len(items))

    def emit(engine, eng):
        for it in lists[eng]:
            for dep in it.deps:
                assert dep.cnt is not None, (it.tag, dep.tag)
                engine.wait_ge(semmap[dep.eng], dep.cnt)
            last = it.fn(engine)
            last.then_inc(semmap[it.eng], step[it.eng])

    with nc.Block() as blk:
        if lists.get("pe"):
            blk.tensor(lambda t: emit(t, "pe"))
        if lists.get("act"):
            blk.scalar(lambda s: emit(s, "act"))
        if lists.get("ve"):
            blk.vector(lambda v: emit(v, "ve"))
        if lists.get("gp"):
            blk.gpsimd(lambda g: emit(g, "gp"))
        if lists.get("sy"):
            blk.sync(lambda s: emit(s, "sy"))


def _gv(C, sq, vec, inst, key="sq"):
    inst.then_inc(sq, 1)
    C.add(key)
    vec.wait_ge(sq, C.get(key))


class Ctr:
    def __init__(self):
        self.v = {}

    def add(self, n, by=1):
        self.v[n] = self.v.get(n, 0) + by
        return self.v[n]

    def get(self, n):
        return self.v.get(n, 0)


def build():
    nc = bass.Bass("TRN2", target_bir_lowering=False)

    x_in = nc.dram_tensor("x_in", [S, D], F32, kind="ExternalInput")
    wq_in = nc.dram_tensor("wq_in", [D, D], F32, kind="ExternalInput")
    wk_in = nc.dram_tensor("wk_in", [D, D], F32, kind="ExternalInput")
    wv_in = nc.dram_tensor("wv_in", [D, D], F32, kind="ExternalInput")
    wo_in = nc.dram_tensor("wo_in", [D, D], F32, kind="ExternalInput")
    wg_in = nc.dram_tensor("wg_in", [D, NEXP], F32, kind="ExternalInput")
    w1_in = nc.dram_tensor("w1_in", [NEXP, D, FF], F32, kind="ExternalInput")
    w2_in = nc.dram_tensor("w2_in", [NEXP, FF, D], F32, kind="ExternalInput")
    b1c_in = nc.dram_tensor("b1c_in", [128, 128], F32, kind="ExternalInput")
    b2_in = nc.dram_tensor("b2_in", [NEXP, D], F32, kind="ExternalInput")
    small_in = nc.dram_tensor("small_in", [1, 256], F32, kind="ExternalInput")
    cos_in = nc.dram_tensor("cos_in", [128, S], F32, kind="ExternalInput")
    sin_in = nc.dram_tensor("sin_in", [128, S], F32, kind="ExternalInput")
    pswap_in = nc.dram_tensor("pswap_in", [128, 96], BF16, kind="ExternalInput")
    ident_in = nc.dram_tensor("ident_in", [128, 128], BF16, kind="ExternalInput")
    esel_in = nc.dram_tensor("esel_in", [8, 8 * 128], BF16, kind="ExternalInput")

    out_t = nc.dram_tensor("out", [S, D], F32, kind="ExternalOutput")
    xspill = nc.dram_tensor("xspill", [S, D], F32, kind="Internal")

    C = Ctr()

    def ent(cm):
        return cm.__enter__(), cm

    def exi(*cms):
        for cm in cms:
            cm.__exit__(None, None, None)

    with ExitStack() as top:
        et = top.enter_context
        consts = et(nc.sbuf_tensor("consts", [128, 64], F32, side="left"))
        gcol = et(nc.sbuf_tensor("gcol", [128, 1], F32, side="left"))
        expbias = et(nc.sbuf_tensor("expbias", [128, 1], F32, side="left"))
        onescol = et(nc.sbuf_tensor("onescol", [128, 1], BF16, side="left"))
        onesrow = et(nc.sbuf_tensor("onesrow", [128, 128], BF16, side="left"))
        neglamrow = et(nc.sbuf_tensor("neglamrow", [128, 64], BF16, side="left"))
        ident = et(nc.sbuf_tensor("ident", [128, 128], BF16, side="left"))
        ident32 = et(nc.sbuf_tensor("ident32", [128, 128], F32, side="left"))
        smallv = et(nc.sbuf_tensor("smallv", [1, 256], F32, side="left"))
        b1c = et(nc.sbuf_tensor("b1c", [128, 128], F32, side="left"))
        b2sb = et(nc.sbuf_tensor("b2sb", [8, D], BF16, side="left"))
        esel = et(nc.sbuf_tensor("esel", [8, 8 * 128], BF16, side="left"))
        pswap = et(nc.sbuf_tensor("pswap", [128, 96], BF16, side="left"))
        bgrow = et(nc.sbuf_tensor("bgrow", [1, 8], BF16, side="left"))
        wgsb = et(nc.sbuf_tensor("wgsb", [128, 64], BF16, side="left"))

        sd = et(nc.semaphore("sd"))
        sp = et(nc.semaphore("sp"))
        sv = et(nc.semaphore("sv"))
        sa = et(nc.semaphore("sa"))
        sg = et(nc.semaphore("sg"))
        so = et(nc.semaphore("so"))
        sq = et(nc.semaphore("sq"))
        sq2 = et(nc.semaphore("sq2"))
        sems = dict(sd=sd, sp=sp, sv=sv, sa=sa, sg=sg, so=so, sq=sq, sq2=sq2)

        tabq_c, cm_tqc = ent(nc.sbuf_tensor("tabq_c", [128, S], BF16, side="right"))
        tabq_s, cm_tqs = ent(nc.sbuf_tensor("tabq_s", [128, S], BF16, side="right"))
        tabk_c, cm_tkc = ent(nc.sbuf_tensor("tabk_c", [128, S], BF16, side="right"))
        tabk_s, cm_tks = ent(nc.sbuf_tensor("tabk_s", [128, S], BF16, side="right"))
        xlnT, cm_xlnT = ent(nc.sbuf_tensor("xlnT", [128, 8 * S], BF16, side="right"))
        xln, cm_xln = ent(nc.sbuf_tensor("xln", [128, 16 * D], BF16, side="right"))
        cosf, cm_cosf = ent(nc.sbuf_tensor("cosf", [128, S], F32, side="right"))
        sinf, cm_sinf = ent(nc.sbuf_tensor("sinf", [128, S], F32, side="right"))
        xres, cm_xres = ent(nc.sbuf_tensor("xres", [128, 16 * D], F32, side="right"))
        lnsc, cm_lnsc = ent(nc.sbuf_tensor("lnsc", [128, 8 * 16], F32, side="right"))
        sqd, cm_sqd = ent(nc.sbuf_tensor("sqd", [128, D], BF16, side="right"))
        grow, cm_grow = ent(nc.sbuf_tensor("grow", [1, 64], BF16, side="right"))

        _sc_ph0 = nc.named_scope("ph0"); _sc_ph0.__enter__()
        with ExitStack() as R:
            ps_s = R.enter_context(nc.psum_tensor("ps_s", [128, 128], F32))

            with nc.Block() as blk:
                @blk.sync
                def _(s):
                    for dst, srcap in [(cosf[:, :], cos_in.ap()),
                                       (sinf[:, :], sin_in.ap()),
                                       (smallv[:, :], small_in.ap()),
                                       (b1c[:, :], b1c_in.ap())]:
                        s.dma_start(dst, srcap).then_inc(sd, 16); C.add("sd", 16)
                    s.dma_start(xres[:, :].rearrange("p (a b) -> p a b", a=16),
                                x_in.ap().rearrange("(a p) b -> p a b", p=128)
                                ).then_inc(sd, 16); C.add("sd", 16)
                    for dst, srcap in [(pswap[:, :], pswap_in.ap()),
                                       (ident[:, :], ident_in.ap()),
                                       (esel[:, :], esel_in.ap())]:
                        s.dma_start(dst, srcap).then_inc(sd, 16); C.add("sd", 16)
                    s.wait_ge(sd, C.get("sd"))

                @blk.gpsimd
                def _(g):
                    g.dma_start(b2sb[:, :], b2_in.ap()).then_inc(sg, 16); C.add("sg", 16)
                    g.dma_start(wgsb[:, :].rearrange("p (a b) -> p a b", a=8),
                                wg_in.ap().rearrange("(a p) b -> p a b", p=128)
                                ).then_inc(sg, 16); C.add("sg", 16)
                    g.wait_ge(sg, C.get("sg"))

            with nc.Block() as blk:
                @blk.vector
                def _(vec):
                    _gv(C, sq, vec, vec.tensor_mul(consts[0:1, 4:36], smallv[0:1, 0:32],
                                   smallv[0:1, 32:64]))
                    _gv(C, sq, vec, vec.reduce_sum(consts[0:1, 0:1], consts[0:1, 4:36], AX.X))
                    _gv(C, sq, vec, vec.tensor_mul(consts[0:1, 4:36], smallv[0:1, 64:96],
                                   smallv[0:1, 96:128]))
                    _gv(C, sq, vec, vec.reduce_sum(consts[0:1, 1:2], consts[0:1, 4:36], AX.X))
                    vec.memset(onescol[:, :], 1.0)
                    vec.memset(onesrow[0:1, :], 1.0)
                    vec.memset(onesrow[64:65, :], 1.0)
                    vec.tensor_scalar_mul(tabq_c[:, :], cosf[:, :], float(SCALING))
                    vec.tensor_scalar_mul(tabq_s[:, :], sinf[:, :], float(SCALING))
                    vec.tensor_copy(tabk_c[:, :], cosf[:, :])
                    vec.tensor_copy(tabk_s[:, :], sinf[:, :])
                    vec.tensor_copy(ident32[:, :], ident[:, :])
                    vec.tensor_copy(grow[0:1, 0:64], smallv[0:1, 128:192])
                    vec.tensor_copy(bgrow[0:1, :], smallv[0:1, 192:200])
            with nc.Block() as blk:
                @blk.scalar
                def _(sc):
                    sc.activation(consts[0:1, 2:3], consts[0:1, 0:1], AF.Exp)
                    sc.activation(consts[0:1, 3:4], consts[0:1, 1:2], AF.Exp)
            with nc.Block() as blk:
                @blk.vector
                def _(vec):
                    _gv(C, sq, vec, vec.tensor_sub(consts[0:1, 0:1], consts[0:1, 2:3],
                                   consts[0:1, 3:4]))
                    _gv(C, sq, vec, vec.tensor_scalar(consts[0:1, 0:1], consts[0:1, 0:1],
                                      float(LAMBDA_INIT), None, ALU.add))
                    _gv(C, sq, vec, vec.tensor_scalar_mul(consts[0:1, 1:2], consts[0:1, 0:1], -1.0))
                    vec.tensor_copy(neglamrow[0:1, 0:1], consts[0:1, 1:2])
            with nc.Block() as blk:
                @blk.tensor
                def _(t):
                    t.matmul(ps_s[64:65, 0:64], neglamrow[0:1, 0:1],
                             onesrow[0:1, 0:64], start=True, stop=True)
                    t.matmul(ps_s[0:64, 64:65], grow[0:1, 0:64],
                             onesrow[0:1, 0:1],
                             start=True, stop=True).then_inc(sp, 1)
                    C.add("sp")

                @blk.vector
                def _(vec):
                    vec.wait_ge(sp, C.get("sp"))
                    vec.tensor_copy(neglamrow[64:65, 0:64], ps_s[64:65, 0:64])
                    vec.tensor_scalar_mul(gcol[0:64, 0:1], ps_s[0:64, 64:65],
                                          float(1.0 - LAMBDA_INIT))

            _layernorm(nc, C, sems, xres, xln, lnsc, sqd)

            with nc.Block() as blk:
                @blk.sync
                def _(s):
                    s.dma_start(xspill.ap().rearrange("(a p) b -> p a b", p=128),
                                xres[:, :].rearrange("p (a b) -> p a b", a=16)
                                ).then_inc(sd, 16); C.add("sd", 16)
                    s.wait_ge(sd, C.get("sd"))

        exi(cm_grow, cm_sqd, cm_lnsc, cm_xres, cm_sinf, cm_cosf)

        _sc_ph0.__exit__(None, None, None)
        _sc_phA = nc.named_scope("phA"); _sc_phA.__enter__()
        q_sb, cm_q = ent(nc.sbuf_tensor("q_sb", [128, NCHUNK * S], BF16, side="left"))
        k_sb, cm_k = ent(nc.sbuf_tensor("k_sb", [128, NCHUNK * S], BF16, side="left"))
        v_sb, cm_v = ent(nc.sbuf_tensor("v_sb", [128, 16 * (NH * 65)], BF16, side="left"))

        with ExitStack() as R:
            pst = [R.enter_context(nc.psum_tensor("t1_%d" % _i, [128, 128], BF16))
                   for _i in range(2)]
            _transpose_tok_feat(nc, C, sems, xln, xlnT, ident, pst)
        exi(cm_xln)

        with ExitStack() as R:
            er = R.enter_context
            wbuf = er(nc.sbuf_tensor("wbuf", [128, 8 * D], BF16, side="right"))
            qraw = er(nc.sbuf_tensor("qraw", [128, S], BF16, side="right"))
            t2b = er(nc.sbuf_tensor("t2b", [128, S], BF16, side="right"))
            ps_p = [er(nc.psum_tensor("t2_%d" % _i, [128, 512], F32)) for _i in range(2)]
            ps_w2 = [er(nc.psum_tensor("t3_%d" % _i, [128, 512], F32)) for _i in range(2)]

            _qk_pipeline(nc, C, sems, wq_in, wbuf, xlnT, q_sb, pswap,
                         tabq_c, tabq_s, qraw, t2b, ps_p, ps_w2)
            _qk_pipeline(nc, C, sems, wk_in, wbuf, xlnT, k_sb, pswap,
                         tabk_c, tabk_s, qraw, t2b, ps_p, ps_w2)
            _v_pipeline(nc, C, sems, wv_in, wbuf, xlnT, v_sb, ps_p)

        with ExitStack() as R:
            er = R.enter_context
            sqb = er(nc.sbuf_tensor("sqb", [128, S], BF16, side="right"))
            mxb = er(nc.sbuf_tensor("mxb", [1, 64], F32, side="right"))
            nbb = er(nc.sbuf_tensor("nbb", [1, 1], BF16, side="right"))
            ps_b = er(nc.psum_tensor("t4", [128, S], F32))
            _bound(nc, C, sems, q_sb, k_sb, onescol, onesrow, sqb, mxb, nbb,
                   ps_b, expbias)

        exi(cm_xlnT, cm_tks, cm_tkc, cm_tqs, cm_tqc)

        _sc_phA.__exit__(None, None, None)
        _sc_phB = nc.named_scope("phB"); _sc_phB.__enter__()
        attn2h, cm_attn = ent(nc.sbuf_tensor("attn2h", [128, 8 * S], BF16,
                                             side="right"))
        _attention(nc, C, sems, q_sb, k_sb, v_sb, attn2h, expbias, onesrow,
                   neglamrow, onescol, gcol, ident)
        exi(cm_v, cm_k, cm_q)

        _sc_phB.__exit__(None, None, None)
        _sc_phC = nc.named_scope("phC"); _sc_phC.__enter__()
        xlnT2 = et(nc.sbuf_tensor("xlnT2", [128, 8 * S], BF16, side="left"))
        combT8 = et(nc.sbuf_tensor("combT8", [8, S], BF16, side="left"))
        y_acc = et(nc.sbuf_tensor("y_acc", [128, 8 * S], BF16, side="left"))

        with ExitStack() as R:
            er = R.enter_context
            wo_sb = er(nc.sbuf_tensor("wo_sb", [128, 8 * D], BF16, side="right"))
            wstage = er(nc.sbuf_tensor("wstage", [128, 512], F32, side="right"))
            xrt = er(nc.sbuf_tensor("xrt", [128, 4 * D], F32, side="right"))
            xl2t = er(nc.sbuf_tensor("xl2t", [128, D], BF16, side="right"))
            lns2 = er(nc.sbuf_tensor("lns2", [128, 8], F32, side="right"))
            gsc = er(nc.sbuf_tensor("gsc", [128, 64], F32, side="right"))
            comb_bf = er(nc.sbuf_tensor("comb_bf", [128, 8], BF16, side="right"))
            ps_w = [er(nc.psum_tensor("t5_%d" % _i, [128, 512], F32)) for _i in range(2)]
            ps_t = [er(nc.psum_tensor("t6_%d" % _i, [128, 128], F32)) for _i in range(2)]
            ps_tb = [er(nc.psum_tensor("t7_%d" % _i, [128, 128], BF16)) for _i in range(2)]
            ps_g = er(nc.psum_tensor("t8", [128, 8], F32))
            ps_8 = er(nc.psum_tensor("t9", [8, 128], F32))

            with nc.Block() as blk:
                @blk.gpsimd
                def _(g):
                    g.dma_start(wo_sb[:, :].rearrange("p (a b) -> p a b", a=8),
                                wo_in.ap().rearrange("(a p) b -> p a b", p=128)
                                ).then_inc(sg, 16); C.add("sg", 16)
                    g.wait_ge(sg, C.get("sg"))

            _wo_ln2_gate(nc, C, sems, wo_sb, attn2h, wstage, xrt, xspill,
                         xl2t, lns2, xlnT2, gsc, comb_bf, combT8, wgsb, bgrow,
                         onesrow, ident, ident32,
                         ps_w, ps_t, ps_tb, ps_g, ps_8)

        exi(cm_attn)

        _sc_phC.__exit__(None, None, None)
        _sc_phE = nc.named_scope("phE"); _sc_phE.__enter__()
        with ExitStack() as R:
            er = R.enter_context
            w1b = er(nc.sbuf_tensor("w1b", [128, 8 * FF], BF16, side="right"))
            w2b = er(nc.sbuf_tensor("w2b", [128, 16 * D], BF16, side="right"))
            w2s = er(nc.sbuf_tensor("w2s", [128, 8 * D], F32, side="right"))
            h_w = er(nc.sbuf_tensor("h_w", [128, 16 * 512], BF16, side="right"))
            h_tmp = er(nc.sbuf_tensor("h_tmp", [128, 512], BF16, side="right"))
            cbc = er(nc.sbuf_tensor("cbc", [128, S], BF16, side="right"))
            ps_h = er(nc.psum_tensor("t10", [128, 4 * 512], F32))
            ps_y = [er(nc.psum_tensor("t11_%d" % _i, [128, 512], F32)) for _i in range(2)]
            ps_c = er(nc.psum_tensor("t12", [128, 512], F32))

            _ffn(nc, C, sems, w1_in, w2_in, xlnT2, w1b, w2b, h_w, h_tmp, cbc,
                 b1c, b2sb, combT8, esel, y_acc, ps_h, ps_y, ps_c, w2s=w2s)

        _sc_phE.__exit__(None, None, None)
        _sc_phF = nc.named_scope("phF"); _sc_phF.__enter__()
        with ExitStack() as R:
            er = R.enter_context
            xtile = [er(nc.sbuf_tensor("t13_%d" % _i, [128, D], F32, side="right"))
                     for _i in range(2)]
            ostage = [er(nc.sbuf_tensor("t14_%d" % _i, [128, D], F32, side="right"))
                      for _i in range(2)]
            ps_o = [er(nc.psum_tensor("t15_%d" % _i, [128, D], BF16)) for _i in range(2)]
            _final(nc, C, sems, y_acc, xspill, out_t, ident, xtile, ostage, ps_o)

    _sc_phF.__exit__(None, None, None)
    return nc



def _layernorm(nc, C, sems, xres, xln, lnsc, sqd):
    sv, sa = sems["sv"], sems["sa"]
    sa0 = C.get("sa")
    with nc.Block() as blk:
        @blk.vector
        def _(vec):
            vec.memset(lnsc[:, :], 0.0).then_inc(sv, 1)
            C.add("sv")
    with nc.Block() as blk:
        @blk.scalar
        def _(sc):
            for tau in range(16):
                sc.activation(sqd[:, :], xres[:, D * tau:D * (tau + 1)],
                              AF.Square,
                              accum_out=lnsc[:, 8 * tau + 1:8 * tau + 2]
                              ).then_inc(sa, 1)
                C.add("sa")

        @blk.vector
        def _(vec):
            sq = sems["sq"]
            for tau in range(16):
                o = 8 * tau
                vec.wait_ge(sa, sa0 + tau + 1)
                _gv(C, sq, vec, vec.reduce_sum(lnsc[:, o:o + 1],
                               xres[:, D * tau:D * (tau + 1)], AX.X))
                _gv(C, sq, vec, vec.tensor_scalar_mul(lnsc[:, o + 2:o + 3],
                               lnsc[:, o:o + 1], 1.0 / D))
                _gv(C, sq, vec, vec.tensor_mul(lnsc[:, o + 3:o + 4],
                               lnsc[:, o + 2:o + 3], lnsc[:, o + 2:o + 3]))
                _gv(C, sq, vec, vec.tensor_scalar(lnsc[:, o + 4:o + 5],
                               lnsc[:, o + 1:o + 2],
                               1.0 / D, float(EPS), ALU.mult, ALU.add))
                vec.tensor_sub(lnsc[:, o + 4:o + 5], lnsc[:, o + 4:o + 5],
                               lnsc[:, o + 3:o + 4]).then_inc(sv, 1)
                C.add("sv")
    sa1 = C.get("sa")
    sq = sems["sq"]
    with nc.Block() as blk:
        @blk.scalar
        def _(sc):
            sq2 = sems["sq2"]
            for tau in range(16):
                o = 8 * tau
                _gv(C, sq2, sc, sc.activation(lnsc[:, o + 5:o + 6],
                               lnsc[:, o + 4:o + 5], AF.Ln), key="sq2")
                sc.activation(lnsc[:, o + 6:o + 7], lnsc[:, o + 5:o + 6],
                              AF.Exp, scale=-0.5).then_inc(sa, 1)
                C.add("sa")

        @blk.vector
        def _(vec):
            for tau in range(16):
                o = 8 * tau
                vec.wait_ge(sa, sa1 + tau + 1)
                vec.tensor_scalar(xln[:, D * tau:D * (tau + 1)],
                                  xres[:, D * tau:D * (tau + 1)],
                                  lnsc[:, o + 2:o + 3], lnsc[:, o + 6:o + 7],
                                  ALU.subtract, ALU.mult)


def _transpose_tok_feat(nc, C, sems, xln, xlnT, ident, pst):
    sp, sv = sems["sp"], sems["sv"]
    sp0, sv0 = C.get("sp"), C.get("sv")
    with nc.Block() as blk:
        @blk.tensor
        def _(t):
            i = 0
            for tau in range(16):
                for fc in range(8):
                    if i >= 2:
                        t.wait_ge(sv, sv0 + i - 1)
                    t.transpose(pst[i % 2][:, :],
                                xln[:, D * tau + 128 * fc:D * tau + 128 * (fc + 1)],
                                ident[:, :]).then_inc(sp, 1)
                    C.add("sp")
                    i += 1

        @blk.vector
        def _(vec):
            i = 0
            for tau in range(16):
                for fc in range(8):
                    vec.wait_ge(sp, sp0 + i + 1)
                    vec.tensor_copy(
                        xlnT[:, S * fc + 128 * tau:S * fc + 128 * (tau + 1)],
                        pst[i % 2][:, :]).then_inc(sv, 1)
                    C.add("sv")
                    i += 1


def _qk_pipeline(nc, C, sems, w_in, wbuf, xlnT, dst, pswap, ctab, stab,
                 qraw, t2b, ps_p, ps_w2):
    sd, sp, sv, sg = sems["sd"], sems["sp"], sems["sv"], sems["sg"]
    with nc.Block() as blk:
        @blk.gpsimd
        def _(g):
            g.dma_start(wbuf[:, :].rearrange("p (a b) -> p a b", a=8),
                        w_in.ap().rearrange("(a p) b -> p a b", p=128)
                        ).then_inc(sg, 16); C.add("sg", 16)
            g.wait_ge(sg, C.get("sg"))

    sp0, sv0 = C.get("sp"), C.get("sv")
    pe_plan = []
    ve_plan = []
    cp, cv = sp0, sv0
    for c in range(NCHUNK):
        m = 96 if c < 10 else 64
        for nt in range(4):
            idx = 4 * c + nt
            wv = (sv0 + _qk_vcount(idx - 2) + 1) if idx >= 2 else None
            pe_plan.append(("proj", c, nt, idx % 2, m, wv))
            cp += 1
            ve_plan.append(("evac", c, nt, idx % 2, m, cp))
            cv += 1
        for nt in range(4):
            sidx = 4 * c + nt
            wv = sv0 + _qk_vcount_chunk_evacs(c)
            wv2 = (sv0 + _qk_vcount_swap(sidx - 2) + 1) if sidx >= 2 else None
            pe_plan.append(("swap", c, nt, nt % 2, m, wv, wv2))
            cp += 1
            ve_plan.append(("ropet2", c, nt, nt % 2, m, cp))
            cv += 1
        ve_plan.append(("ropefin", c, 0, 0, m, cp))
        cv += 1

    with nc.Block() as blk:
        @blk.tensor
        def _(t):
            for item in pe_plan:
                if item[0] == "proj":
                    _, c, nt, slot, m, wv = item
                    if wv is not None:
                        t.wait_ge(sv, wv)
                    for kc in range(8):
                        mm = t.matmul(
                            ps_p[slot][0:m, :],
                            wbuf[:, D * kc + 96 * c:D * kc + 96 * c + m],
                            xlnT[:, S * kc + 512 * nt:S * kc + 512 * (nt + 1)],
                            start=(kc == 0), stop=(kc == 7))
                    mm.then_inc(sp, 1); C.add("sp")
                else:
                    _, c, nt, slot, m, wv, wv2 = item
                    t.wait_ge(sv, wv)
                    if wv2 is not None:
                        t.wait_ge(sv, wv2)
                    t.matmul(ps_w2[slot][0:m, :], pswap[0:m, 0:m],
                             qraw[0:m, 512 * nt:512 * (nt + 1)],
                             start=True, stop=True).then_inc(sp, 1)
                    C.add("sp")

        @blk.vector
        def _(vec):
            for item in ve_plan:
                kind, c, nt, slot, m = item[0], item[1], item[2], item[3], item[4]
                if kind == "evac":
                    vec.wait_ge(sp, item[5])
                    vec.tensor_copy(qraw[0:m, 512 * nt:512 * (nt + 1)],
                                    ps_p[slot][0:m, :]).then_inc(sv, 1)
                    C.add("sv")
                elif kind == "ropet2":
                    vec.wait_ge(sp, item[5])
                    vec.tensor_tensor(t2b[0:m, 512 * nt:512 * (nt + 1)],
                                      ps_w2[slot][0:m, :],
                                      stab[0:m, 512 * nt:512 * (nt + 1)],
                                      ALU.mult).then_inc(sv, 1)
                    C.add("sv")
                else:
                    vec.tensor_tensor(qraw[0:m, :], qraw[0:m, :], ctab[0:m, :],
                                      ALU.mult)
                    vec.tensor_tensor(dst[0:m, S * c:S * (c + 1)],
                                      qraw[0:m, :], t2b[0:m, :],
                                      ALU.add).then_inc(sv, 1)
                    C.add("sv")

    assert C.get("sp") == cp and C.get("sv") == cv, (C.get("sp"), cp, C.get("sv"), cv)


def _qk_vcount(proj_idx):
    c, nt = divmod(proj_idx, 4)
    return 9 * c + nt


def _qk_vcount_chunk_evacs(c):
    return 9 * c + 4


def _qk_vcount_swap(sidx):
    c, nt = divmod(sidx, 4)
    return 9 * c + 4 + nt


def _v_pipeline(nc, C, sems, wv_in, wbuf, xlnT, v_sb, ps_p):
    sd, sp, sv, sg = sems["sd"], sems["sp"], sems["sv"], sems["sg"]
    with nc.Block() as blk:
        @blk.gpsimd
        def _(g):
            g.dma_start(wbuf[:, :].rearrange("p (a b) -> p a b", a=8),
                        wv_in.ap().rearrange("(a p) b -> p a b", p=128)
                        ).then_inc(sg, 16); C.add("sg", 16)
            g.wait_ge(sg, C.get("sg"))

        @blk.vector
        def _(vec):
            vec.memset(
                v_sb[:, :].rearrange("p (a b c) -> p a b c", a=16, b=NH)[:, :, :, 64:65],
                1.0)

    sp0, sv0 = C.get("sp"), C.get("sv")
    with nc.Block() as blk:
        @blk.tensor
        def _(t):
            for j in range(16):
                for nt in range(2):
                    idx = 2 * j + nt
                    if idx >= 2:
                        t.wait_ge(sv, sv0 + idx - 1)
                    for kc in range(8):
                        mm = t.matmul(
                            ps_p[idx % 2][:, :],
                            xlnT[:, S * kc + 128 * j:S * kc + 128 * (j + 1)],
                            wbuf[:, D * kc + 512 * nt:D * kc + 512 * (nt + 1)],
                            start=(kc == 0), stop=(kc == 7))
                    mm.then_inc(sp, 1); C.add("sp")

        @blk.vector
        def _(vec):
            for j in range(16):
                for nt in range(2):
                    idx = 2 * j + nt
                    vec.wait_ge(sp, sp0 + idx + 1)
                    dst = v_sb[:, :].rearrange("p (a b c) -> p a b c",
                                               a=16, b=NH)[:, j, 8 * nt:8 * (nt + 1), 0:64]
                    vec.tensor_copy(dst,
                                    ps_p[idx % 2][:, :].rearrange(
                                        "p (h d) -> p h d", h=8)
                                    ).then_inc(sv, 1)
                    C.add("sv")


def _bound(nc, C, sems, q_sb, k_sb, onescol, onesrow, sqb, mxb, nbb, ps_b,
           expbias):
    sp, sv, sa = sems["sp"], sems["sv"], sems["sa"]
    for wi, src in enumerate([q_sb, k_sb]):
        sa0 = C.get("sa")
        sp0 = C.get("sp")
        sv0 = C.get("sv")
        with nc.Block() as blk:
            @blk.scalar
            def _(sc):
                for c in range(NCHUNK):
                    m = 96 if c < 10 else 64
                    if c >= 1:
                        sc.wait_ge(sp, sp0 + c)
                    sc.activation(sqb[0:m, :], src[0:m, S * c:S * (c + 1)],
                                  AF.Square).then_inc(sa, 1)
                    C.add("sa")

            @blk.tensor
            def _(t):
                for c in range(NCHUNK):
                    m = 96 if c < 10 else 64
                    t.wait_ge(sa, sa0 + c + 1)
                    if c >= 1:
                        t.wait_ge(sv, sv0 + c)
                    for nt in range(4):
                        mm = t.matmul(ps_b[0:1, 512 * nt:512 * (nt + 1)],
                                      onescol[0:m, 0:1],
                                      sqb[0:m, 512 * nt:512 * (nt + 1)],
                                      start=True, stop=True)
                    mm.then_inc(sp, 1); C.add("sp")

            @blk.vector
            def _(vec):
                for c in range(NCHUNK):
                    vec.wait_ge(sp, sp0 + c + 1)
                    vec.reduce_max(mxb[0:1, 32 * wi + c:32 * wi + c + 1],
                                   ps_b[0:1, :], AX.X).then_inc(sv, 1)
                    C.add("sv")

    with nc.Block() as blk:
        @blk.vector
        def _(vec):
            sq = sems["sq"]
            _gv(C, sq, vec, vec.reduce_max(mxb[0:1, 20:21], mxb[0:1, 0:11], AX.X))
            _gv(C, sq, vec, vec.reduce_max(mxb[0:1, 21:22], mxb[0:1, 32:43], AX.X))
            vec.tensor_mul(mxb[0:1, 22:23], mxb[0:1, 20:21], mxb[0:1, 21:22])
    with nc.Block() as blk:
        @blk.scalar
        def _(sc):
            sq2 = sems["sq2"]
            _gv(C, sq2, sc, sc.activation(mxb[0:1, 24:25], mxb[0:1, 22:23], AF.Ln),
                key="sq2")
            sc.activation(mxb[0:1, 23:24], mxb[0:1, 24:25], AF.Exp, scale=0.5)
    with nc.Block() as blk:
        @blk.vector
        def _(vec):
            vec.tensor_scalar_mul(nbb[0:1, 0:1], mxb[0:1, 23:24], -1.0)
    with nc.Block() as blk:
        @blk.tensor
        def _(t):
            t.matmul(ps_b[:, 0:1], onesrow[0:1, 0:128], nbb[0:1, 0:1],
                     start=True, stop=True).then_inc(sp, 1)
            C.add("sp")

        @blk.vector
        def _(vec):
            vec.wait_ge(sp, C.get("sp"))
            vec.tensor_copy(expbias[:, 0:1], ps_b[:, 0:1])


def _attention(nc, C, sems, q_sb, k_sb, v_sb, attn2h, expbias, onesrow,
               neglamrow, onescol, gcol, ident):
    sq = sems["sq"]
    with ExitStack() as R:
        er = R.enter_context
        probs = [er(nc.sbuf_tensor("at_pr%d" % _i, [128, 3 * 512], BF16, side="right"))
                 for _i in range(2)]
        u_sb = er(nc.sbuf_tensor("at_u", [128, 6 * 512], F32, side="right"))
        zrf = er(nc.sbuf_tensor("at_zrf", [128, 1024], F32, side="right"))
        zr = er(nc.sbuf_tensor("at_zr", [128, 1024], BF16, side="right"))
        bca = er(nc.sbuf_tensor("at_bca", [64, 512], BF16, side="right"))
        bcb = er(nc.sbuf_tensor("at_bcb", [64, 512], BF16, side="right"))
        tmp1 = er(nc.sbuf_tensor("at_t1", [64, 512], BF16, side="right"))
        tmp2 = er(nc.sbuf_tensor("at_t2", [64, 512], BF16, side="right"))
        comb_o = er(nc.sbuf_tensor("at_co", [64, 512], BF16, side="right"))
        sqs = er(nc.sbuf_tensor("at_sq", [64, 512], BF16, side="right"))
        rmsr = er(nc.sbuf_tensor("at_rr", [1, 1024], F32, side="right"))
        rms_bf = er(nc.sbuf_tensor("at_rb", [1, 512], BF16, side="right"))
        mvd = er(nc.sbuf_tensor("at_mv", [64, 512], BF16, side="right"))
        ps_sc = er(nc.psum_tensor("at_psc", [128, 3 * 512], F32))
        ps_pv = [er(nc.psum_tensor("at_ppv%d" % _i, [65, 512], F32))
                 for _i in range(3)]
        ps_bc = er(nc.psum_tensor("at_pbc", [128, 512], F32))
        ps_bc2 = er(nc.psum_tensor("at_pbc2", [128, 512], F32))

        L = dict(pe=[], act=[], ve=[], gp=[])
        pend_pe, pend_act = [], []
        free_slots = [0, 1, 2]
        slot_evac = {0: None, 1: None, 2: None}
        free_uslots = [0, 1, 2, 3, 4, 5]
        uslot_free_item = {u: None for u in range(6)}
        hold = {}
        last_exp = None
        last_move = None
        prev_pe2 = None
        prev_v4 = None
        prev_v2 = None

        def vslice(j, h):
            return v_sb[:, :].rearrange("p (a b) -> p a b", a=16)[:, j,
                                                                 65 * h:65 * (h + 1)]

        for qt in range(4):
            for c in range(NCHUNK):
                subs = list(range(3 * c, min(3 * c + 3, NSUB)))
                nsp = len(subs)
                slots = [free_slots.pop(0) for _ in subs]
                uslots = [free_uslots.pop(0) for _ in subs]
                for s_, ul in zip(subs, uslots):
                    hold[s_] = ul

                pv_items = []
                pend_pv = None
                for j in range(16):
                    deps = []
                    if last_exp is not None:
                        deps.append(last_exp)
                    if j == 0:
                        for sl in slots:
                            if slot_evac[sl] is not None:
                                deps.append(slot_evac[sl])

                    def f_scores(t, qt=qt, c=c, j=j, subs=subs):
                        for i_s, s_ in enumerate(subs):
                            spp = s_ - 3 * c
                            mm = t.matmul(
                                ps_sc[:, 512 * i_s:512 * (i_s + 1)],
                                k_sb[32 * spp:32 * (spp + 1),
                                     S * c + 128 * j:S * c + 128 * (j + 1)],
                                q_sb[32 * spp:32 * (spp + 1),
                                     S * c + 512 * qt:S * c + 512 * (qt + 1)],
                                start=True, stop=True,
                                tile_position=(32 * spp, 0))
                        return mm
                    it_sc = _It("pe", f_scores, deps, "sc%d_%d_%d" % (qt, c, j))
                    L["pe"].append(it_sc)

                    def f_exp(s, j=j, nsp=nsp):
                        return s.activation(probs[j % 2][:, 0:512 * nsp],
                                            ps_sc[:, 0:512 * nsp], AF.Exp,
                                            bias=expbias[:, 0:1])
                    it_exp = _It("act", f_exp, [it_sc], "ex%d_%d_%d" % (qt, c, j))
                    L["act"].append(it_exp)
                    last_exp = it_exp

                    def f_pv(t, qt=qt, c=c, j=j, subs=subs, slots=slots):
                        for i_s, s_ in enumerate(subs):
                            mm = t.matmul(
                                ps_pv[slots[i_s]][0:65, :],
                                vslice(j, s_ // 2),
                                probs[j % 2][:, 512 * i_s:512 * (i_s + 1)],
                                start=(j == 0), stop=(j == 15))
                        return mm
                    it_pv = _It("pe", f_pv, [it_exp], "pv%d_%d_%d" % (qt, c, j))
                    pv_items.append(it_pv)
                    if pend_pv is not None:
                        L["pe"].append(pend_pv)
                    pend_pv = it_pv
                    while pend_pe and pend_pe[0][0] <= j:
                        L["pe"].append(pend_pe.pop(0)[1])
                    while pend_act and pend_act[0][0] <= j:
                        L["act"].append(pend_act.pop(0)[1])



                if pend_pv is not None:
                    L["pe"].append(pend_pv)
                    pend_pv = None
                ev_deps = [pv_items[-1]]
                for ul in uslots:
                    if uslot_free_item[ul] is not None:
                        ev_deps.append(uslot_free_item[ul])

                def f_evac(v, subs=subs, slots=slots, uslots=uslots):
                    for i_s in range(len(subs)):
                        last = v.tensor_copy(u_sb[0:65, 512 * uslots[i_s]:
                                                  512 * uslots[i_s] + 512],
                                             ps_pv[slots[i_s]][0:65, :])
                    return last
                it_ev = _It("ve", f_evac, ev_deps, "ev%d_%d" % (qt, c))
                L["ve"].append(it_ev)
                for sl in slots:
                    slot_evac[sl] = it_ev
                free_slots.extend(slots)

                kpair = 0
                for s_ in subs:
                    if s_ % 2 != 1:
                        continue
                    sA, sB = hold[s_ - 1], hold[s_]
                    h = s_ // 2
                    del hold[s_ - 1], hold[s_]

                    def f_act1(s, sA=sA, sB=sB):
                        s.activation(zrf[64:65, 0:512],
                                     u_sb[64:65, 512 * sA:512 * sA + 512], AF.Ln)
                        i2 = s.activation(zrf[64:65, 512:1024],
                                          u_sb[64:65, 512 * sB:512 * sB + 512],
                                          AF.Ln)
                        _gv(C, sems["sq2"], s, i2, key="sq2")
                        return s.activation(zr[64:65, 0:1024], zrf[64:65, 0:1024],
                                            AF.Exp, scale=-1.0)
                    d1 = [it_ev] + ([prev_pe2] if prev_pe2 is not None else [])
                    it_a1 = _It("act", f_act1, d1, "a1_%d_%d" % (qt, h))

                    def f_bca(t):
                        return t.matmul(ps_bc[0:64, :], onesrow[64:65, 0:64],
                                        zr[64:65, 0:512], start=True, stop=True)
                    d_p1 = [it_a1] + ([prev_v4] if prev_v4 is not None else [])
                    it_p1 = _It("pe", f_bca, d_p1, "p1_%d_%d" % (qt, h))

                    def f_bcb(t):
                        return t.matmul(ps_bc2[0:64, :], neglamrow[64:65, 0:64],
                                        zr[64:65, 512:1024], start=True, stop=True)
                    d_p2 = [it_a1] + ([prev_v2] if prev_v2 is not None else [])
                    it_p2 = _It("pe", f_bcb, d_p2, "p2_%d_%d" % (qt, h))
                    prev_pe2 = it_p2

                    def f_comb(v, sA=sA, sB=sB):
                        v.tensor_tensor(tmp1[0:64, :],
                                        u_sb[0:64, 512 * sA:512 * sA + 512],
                                        ps_bc[0:64, :], ALU.mult)
                        v.tensor_tensor(tmp2[0:64, :],
                                        u_sb[0:64, 512 * sB:512 * sB + 512],
                                        ps_bc2[0:64, :], ALU.mult)
                        i4 = v.tensor_tensor(comb_o[0:64, :], tmp1[0:64, :],
                                             tmp2[0:64, :], ALU.add)
                        _gv(C, sq, v, i4)
                        return v.tensor_tensor(sqs[0:64, :], comb_o[0:64, :],
                                               comb_o[0:64, :], ALU.mult)
                    it_v2 = _It("ve", f_comb, [it_p1, it_p2], "v2_%d_%d" % (qt, h))
                    prev_v2 = it_v2

                    def f_sel(t):
                        return t.matmul(ps_bc[0:1, :], onescol[0:64, 0:1],
                                        sqs[0:64, :], start=True, stop=True)
                    it_p3 = _It("pe", f_sel, [it_v2], "p3_%d_%d" % (qt, h))

                    def f_rp(v):
                        return v.tensor_scalar(rmsr[0:1, 0:512], ps_bc[0:1, :],
                                               1.0 / 64, float(EPS), ALU.mult,
                                               ALU.add)
                    it_v3 = _It("ve", f_rp, [it_p3], "v3_%d_%d" % (qt, h))

                    def f_act2(s):
                        i1 = s.activation(rmsr[0:1, 512:1024], rmsr[0:1, 0:512],
                                          AF.Ln)
                        _gv(C, sems["sq2"], s, i1, key="sq2")
                        return s.activation(rms_bf[0:1, :], rmsr[0:1, 512:1024],
                                            AF.Exp, scale=-0.5)
                    it_a2 = _It("act", f_act2, [it_v3], "a2_%d_%d" % (qt, h))

                    def f_rbc(t):
                        return t.matmul(ps_bc[0:64, :], onesrow[0:1, 0:64],
                                        rms_bf[0:1, :], start=True, stop=True)
                    it_p4 = _It("pe", f_rbc, [it_a2], "p4_%d_%d" % (qt, h))

                    def f_apply(v, h=h, qt=qt):
                        v.tensor_scalar(mvd[0:64, :] if False else tmp1[0:64, :],
                                        ps_bc[0:64, :], gcol[0:64, 0:1], None,
                                        ALU.mult)
                        if h % 2 == 0:
                            return v.tensor_tensor(
                                attn2h[0:64, S * (h // 2) + 512 * qt:
                                       S * (h // 2) + 512 * (qt + 1)],
                                comb_o[0:64, :], tmp1[0:64, :], ALU.mult)
                        return v.tensor_tensor(mvd[0:64, :], comb_o[0:64, :],
                                               tmp1[0:64, :], ALU.mult)
                    d4 = [it_p4] + ([last_move] if (h % 2 == 1 and
                                                    last_move is not None) else [])
                    it_v4 = _It("ve", f_apply, d4, "v4_%d_%d" % (qt, h))

                    L["ve"].extend([it_v2, it_v3, it_v4])
                    b_ = 6 * kpair
                    pend_act.extend([(b_ + 0, it_a1), (b_ + 3, it_a2)])
                    pend_pe.extend([(b_ + 1, it_p1), (b_ + 1, it_p2),
                                    (b_ + 2, it_p3), (b_ + 4, it_p4)])
                    kpair += 1
                    prev_v4 = it_v4
                    for ul in (sA, sB):
                        uslot_free_item[ul] = it_v2
                        free_uslots.append(ul)

                    if h % 2 == 1:
                        def f_move(g, h=h, qt=qt):
                            return g.dma_start(
                                attn2h[64:128, S * (h // 2) + 512 * qt:
                                       S * (h // 2) + 512 * (qt + 1)],
                                mvd[0:64, :])
                        it_gp = _It("gp", f_move, [it_v4], "mv%d_%d" % (qt, h))
                        L["gp"].append(it_gp)
                        last_move = it_gp
        L["pe"].extend(it for _, it in pend_pe)
        L["act"].extend(it for _, it in pend_act)
        _run_items(nc, C, sems, L)
        with nc.Block() as blk:
            @blk.gpsimd
            def _(g):
                g.wait_ge(sems["sg"], C.get("sg"))


def _wo_ln2_gate(nc, C, sems, wo_sb, attn2h, wstage, xrt, xspill, xl2t, lns2,
                 xlnT2, gsc, comb_bf, combT8, wgsb, bgrow, onesrow, ident,
                 ident32, ps_w, ps_t, ps_tb, ps_g, ps_8):
    sd, sp, sv, sa = sems["sd"], sems["sp"], sems["sv"], sems["sa"]

    for qt in range(4):
        with nc.Block() as blk:
            @blk.sync
            def _(s):
                s.wait_ge(sd, C.get("sd"))
                for qi in range(4):
                    tau = 4 * qt + qi
                    s.dma_start(xrt[:, D * qi:D * (qi + 1)],
                                xspill.ap()[128 * tau:128 * (tau + 1), :]
                                ).then_inc(sd, 16); C.add("sd", 16)
                s.wait_ge(sd, C.get("sd"))

        sp0, sv0 = C.get("sp"), C.get("sv")
        pe_plan, ve_plan = [], []
        cp, cv = sp0, sv0
        for dout in range(8):
            pe_plan.append(("wo", dout))
            cp += 1
            ve_plan.append(("evac", dout, cp))
            cv += 1
            for qi in range(4):
                pe_plan.append(("tr", dout, qi))
                cp += 1
                ve_plan.append(("add", dout, qi, cp))
                cv += 1
        def pe_idx(kind, d, qi=0):
            if kind == "wo":
                return 5 * d + 1
            return 5 * d + 2 + qi

        with nc.Block() as blk:
            @blk.tensor
            def _(t):
                for item in pe_plan:
                    if item[0] == "wo":
                        d = item[1]
                        if d >= 2:
                            t.wait_ge(sv, sv0 + 5 * (d - 2) + 1)
                        for hp in range(8):
                            mm = t.matmul(
                                ps_w[d % 2][:, :],
                                wo_sb[:, D * hp + 128 * d:D * hp + 128 * (d + 1)],
                                attn2h[:, S * hp + 512 * qt:S * hp + 512 * (qt + 1)],
                                start=(hp == 0), stop=(hp == 7))
                        mm.then_inc(sp, 1); C.add("sp")
                    else:
                        d, qi = item[1], item[2]
                        t.wait_ge(sv, sv0 + 5 * d + 1)
                        if qi >= 2:
                            t.wait_ge(sv, sv0 + 5 * d + 2 + (qi - 2))
                        elif d >= 1:
                            t.wait_ge(sv, sv0 + 5 * (d - 1) + 2 + qi + 2)
                        t.transpose(ps_t[qi % 2][:, :],
                                    wstage[:, 128 * qi:128 * (qi + 1)],
                                    ident32[:, :]).then_inc(sp, 1)
                        C.add("sp")

            @blk.vector
            def _(vec):
                for item in ve_plan:
                    if item[0] == "evac":
                        d = item[1]
                        vec.wait_ge(sp, sp0 + pe_idx("wo", d))
                        if d >= 1:
                            vec.wait_ge(sp, sp0 + pe_idx("tr", d - 1, 3))
                        vec.tensor_copy(wstage[:, :], ps_w[d % 2][:, :]
                                        ).then_inc(sv, 1)
                        C.add("sv")
                    else:
                        d, qi = item[1], item[2]
                        vec.wait_ge(sp, sp0 + pe_idx("tr", d, qi))
                        vec.tensor_tensor(
                            xrt[:, D * qi + 128 * d:D * qi + 128 * (d + 1)],
                            xrt[:, D * qi + 128 * d:D * qi + 128 * (d + 1)],
                            ps_t[qi % 2][:, :], ALU.add).then_inc(sv, 1)
                        C.add("sv")

        sq = sems["sq"]
        sq2 = sems["sq2"]
        Lt = dict(pe=[], act=[], ve=[], sy=[])
        prev = dict(v_nm=None, a_le=None, p_tr_last=None, v_te=[None] * 8,
                    v_g1=None, v_ce=None)
        for qi in range(4):
            tau = 4 * qt + qi
            xt = xrt[:, D * qi:D * (qi + 1)]

            def f_ms(v):
                return v.memset(lns2[:, 1:2], 0.0)
            dms = [x for x in (prev["v_nm"], prev["a_le"]) if x is not None]
            i_ms = _It("ve", f_ms, dms, "cms%d" % tau)
            Lt["ve"].append(i_ms)

            def f_sq(s, xt=xt):
                return s.activation(xl2t[:, :], xt, AF.Square,
                                    accum_out=lns2[:, 1:2])
            dsq = [i_ms] + ([prev["p_tr_last"]] if prev["p_tr_last"] else [])
            i_sq = _It("act", f_sq, dsq, "csq%d" % tau)
            Lt["act"].append(i_sq)

            def f_st(v, xt=xt):
                _gv(C, sq, v, v.reduce_sum(lns2[:, 0:1], xt, AX.X))
                _gv(C, sq, v, v.tensor_scalar_mul(lns2[:, 2:3], lns2[:, 0:1],
                                                  1.0 / D))
                _gv(C, sq, v, v.tensor_mul(lns2[:, 3:4], lns2[:, 2:3],
                                           lns2[:, 2:3]))
                _gv(C, sq, v, v.tensor_scalar(lns2[:, 4:5], lns2[:, 1:2],
                                              1.0 / D, float(EPS), ALU.mult,
                                              ALU.add))
                return v.tensor_sub(lns2[:, 4:5], lns2[:, 4:5], lns2[:, 3:4])
            i_st = _It("ve", f_st, [i_sq], "cst%d" % tau)
            Lt["ve"].append(i_st)

            def f_le(s):
                _gv(C, sq2, s, s.activation(lns2[:, 5:6], lns2[:, 4:5], AF.Ln),
                    key="sq2")
                return s.activation(lns2[:, 6:7], lns2[:, 5:6], AF.Exp,
                                    scale=-0.5)
            i_le = _It("act", f_le, [i_st], "cle%d" % tau)
            Lt["act"].append(i_le)
            prev["a_le"] = i_le

            def f_nm(v, xt=xt):
                return v.tensor_scalar(xl2t[:, :], xt, lns2[:, 2:3],
                                       lns2[:, 6:7], ALU.subtract, ALU.mult)
            i_nm = _It("ve", f_nm, [i_le], "cnm%d" % tau)
            Lt["ve"].append(i_nm)
            prev["v_nm"] = i_nm

            def f_sp(s, xt=xt, tau=tau):
                return s.dma_start(xspill.ap()[128 * tau:128 * (tau + 1), :], xt)
            Lt["sy"].append(_It("sy", f_sp, [i_nm], "csp%d" % tau))

            for fc in range(8):
                def f_tr(t, fc=fc):
                    return t.transpose(ps_tb[fc % 2][:, :],
                                       xl2t[:, 128 * fc:128 * (fc + 1)],
                                       ident[:, :])
                dtr = [i_nm]
                if prev["v_te"][fc - 2] is not None:
                    dtr.append(prev["v_te"][fc - 2])
                i_tr = _It("pe", f_tr, dtr, "ctr%d_%d" % (tau, fc))
                Lt["pe"].append(i_tr)

                def f_te(v, fc=fc, tau=tau):
                    return v.tensor_copy(
                        xlnT2[:, S * fc + 128 * tau:S * fc + 128 * (tau + 1)],
                        ps_tb[fc % 2][:, :])
                i_te = _It("ve", f_te, [i_tr], "cte%d_%d" % (tau, fc))
                Lt["ve"].append(i_te)
                prev["v_te"][fc] = i_te
            prev["p_tr_last"] = i_tr

            def f_g(t, tau=tau):
                for kc in range(8):
                    t.matmul(ps_g[:, :],
                             xlnT2[:, S * kc + 128 * tau:S * kc + 128 * (tau + 1)],
                             wgsb[:, 8 * kc:8 * (kc + 1)],
                             start=(kc == 0), stop=False)
                return t.matmul(ps_g[:, :], onesrow[0:1, 0:128], bgrow[0:1, :],
                                start=False, stop=True)
            dg = [prev["v_te"][7]] + ([prev["v_g1"]] if prev["v_g1"] else [])
            i_g = _It("pe", f_g, dg, "cg%d" % tau)
            Lt["pe"].append(i_g)

            def f_g1(v):
                _gv(C, sq, v, v.reduce_max(gsc[:, 0:1], ps_g[:, :], AX.X))
                return v.tensor_single_scalar(gsc[:, 8:16], ps_g[:, :],
                                              gsc[:, 0:1], ALU.subtract)
            i_g1 = _It("ve", f_g1, [i_g], "cg1%d" % tau)
            Lt["ve"].append(i_g1)
            prev["v_g1"] = i_g1

            def f_ex(s):
                return s.activation(gsc[:, 16:24], gsc[:, 8:16], AF.Exp)
            i_ex = _It("act", f_ex, [i_g1], "cex%d" % tau)
            Lt["act"].append(i_ex)

            def f_g2(v):
                e = gsc[:, 16:24]
                m1, m2 = gsc[:, 1:2], gsc[:, 2:3]
                mask1, gm, mask2 = gsc[:, 24:32], gsc[:, 32:40], gsc[:, 40:48]
                den, dinv = gsc[:, 3:4], gsc[:, 4:5]
                _gv(C, sq, v, v.reduce_max(m1, e, AX.X))
                _gv(C, sq, v, v.tensor_single_scalar(mask1, e, m1, ALU.is_ge))
                _gv(C, sq, v, v.scalar_tensor_tensor(gm, mask1, -1e30, e,
                                                     ALU.mult, ALU.add))
                _gv(C, sq, v, v.reduce_max(m2, gm, AX.X))
                _gv(C, sq, v, v.tensor_single_scalar(mask2, e, m2, ALU.is_ge))
                _gv(C, sq, v, v.tensor_add(den, m1, m2))
                _gv(C, sq, v, v.reciprocal(dinv, den))
                _gv(C, sq, v, v.scalar_tensor_tensor(gsc[:, 48:56], e, dinv,
                                                     mask2, ALU.mult, ALU.mult))
                return v.tensor_copy(comb_bf[:, :], gsc[:, 48:56])
            i_g2 = _It("ve", f_g2, [i_ex], "cg2%d" % tau)
            Lt["ve"].append(i_g2)

            def f_p8(t):
                return t.matmul(ps_8[0:8, :], comb_bf[:, 0:8], ident[:, :],
                                start=True, stop=True)
            dp8 = [i_g2] + ([prev["v_ce"]] if prev["v_ce"] else [])
            i_p8 = _It("pe", f_p8, dp8, "cp8%d" % tau)
            Lt["pe"].append(i_p8)

            def f_ce(v, tau=tau):
                return v.tensor_copy(combT8[0:8, 128 * tau:128 * (tau + 1)],
                                     ps_8[0:8, :])
            i_ce = _It("ve", f_ce, [i_p8], "cce%d" % tau)
            Lt["ve"].append(i_ce)
            prev["v_ce"] = i_ce
        _run_items(nc, C, sems, Lt)


def _ln2_tile(nc, C, sems, xt, xl2t, lns2):
    sv, sa = sems["sv"], sems["sa"]
    with nc.Block() as blk:
        @blk.vector
        def _(vec):
            vec.memset(lns2[:, 1:2], 0.0)
    with nc.Block() as blk:
        @blk.scalar
        def _(sc):
            sc.activation(xl2t[:, :], xt, AF.Square,
                          accum_out=lns2[:, 1:2]).then_inc(sa, 1)
            C.add("sa")

        @blk.vector
        def _(vec):
            sq = sems["sq"]
            vec.wait_ge(sa, C.get("sa"))
            _gv(C, sq, vec, vec.reduce_sum(lns2[:, 0:1], xt, AX.X))
            _gv(C, sq, vec, vec.tensor_scalar_mul(lns2[:, 2:3], lns2[:, 0:1], 1.0 / D))
            _gv(C, sq, vec, vec.tensor_mul(lns2[:, 3:4], lns2[:, 2:3], lns2[:, 2:3]))
            _gv(C, sq, vec, vec.tensor_scalar(lns2[:, 4:5], lns2[:, 1:2], 1.0 / D,
                              float(EPS), ALU.mult, ALU.add))
            vec.tensor_sub(lns2[:, 4:5], lns2[:, 4:5], lns2[:, 3:4]
                           ).then_inc(sv, 1)
            C.add("sv")
    with nc.Block() as blk:
        @blk.scalar
        def _(sc):
            sq2 = sems["sq2"]
            _gv(C, sq2, sc, sc.activation(lns2[:, 5:6], lns2[:, 4:5], AF.Ln),
                key="sq2")
            sc.activation(lns2[:, 6:7], lns2[:, 5:6], AF.Exp,
                          scale=-0.5).then_inc(sa, 1)
            C.add("sa")

        @blk.vector
        def _(vec):
            vec.wait_ge(sa, C.get("sa"))
            vec.tensor_scalar(xl2t[:, :], xt, lns2[:, 2:3], lns2[:, 6:7],
                              ALU.subtract, ALU.mult)


def _ffn(nc, C, sems, w1_in, w2_in, xlnT2, w1b, w2b, h_w, h_tmp, cbc,
         b1c, b2sb, combT8, esel, y_acc, ps_h, ps_y, ps_c, w2s=None):
    with nc.Block() as blk:
        @blk.vector
        def _(vec):
            vec.memset(y_acc[:, :], 0.0)

    L = dict(pe=[], act=[], ve=[], gp=[], sy=[])
    last_h = {}
    last_y = {}
    relu_items = []
    cmult_items = []
    yacc_items = []
    last_cmult_of_expert = {}
    gp_w1 = {}
    gp_w2 = {}
    prev_cast = None

    for e in range(NEXP):
        def f_w1(g, e=e):
            return g.dma_start(w1b[:, :].rearrange("p (a b) -> p a b", a=8),
                               w1_in.ap()[e].rearrange("(a p) b -> p a b", p=128))
        gp_w1[e] = _It("gp", f_w1,
                       [last_h[e - 1]] if e > 0 else [], "w1_%d" % e)
        L["gp"].append(gp_w1[e])

        cast_items = []
        for hf in range(2):
            def f_w2s(s, e=e, hf=hf):
                return s.dma_start(
                    w2s[:, :].rearrange("p (a b) -> p a b", a=8),
                    w2_in.ap()[e][1024 * hf:1024 * (hf + 1), :].rearrange(
                        "(a p) b -> p a b", p=128)) if hf == 0 else s.dma_start(
                    w2s[:, :].rearrange("p (a b) -> p a b", a=8),
                    w2_in.ap()[e][1024:2048, :].rearrange(
                        "(a p) b -> p a b", p=128))
            d_s = []
            if cast_items:
                d_s.append(cast_items[-1])
            elif prev_cast is not None:
                d_s.append(prev_cast)
            if hf == 0 and e > 0:
                d_s.append(last_y[e - 1])
            it_ws = _It("sy", f_w2s, d_s, "w2s%d_%d" % (e, hf))
            L["sy"].append(it_ws)

            def f_cast(v, hf=hf):
                return v.tensor_copy(
                    w2b[:, 8 * D * hf:8 * D * (hf + 1)], w2s[:, :])
            d_c = [it_ws]
            if e > 0 and hf == 0:
                d_c.append(last_y[e - 1])
            it_c = _It("ve", f_cast, d_c, "w2c%d_%d" % (e, hf))
            L["ve"].append(it_c)
            cast_items.append(it_c)
        prev_cast = cast_items[-1]
        gp_w2[e] = cast_items[-1]

        cb_ev = []
        for nt in range(4):
            def f_cb(t, e=e, nt=nt):
                return t.matmul(ps_c[:, :], esel[0:8, 128 * e:128 * (e + 1)],
                                combT8[0:8, 512 * nt:512 * (nt + 1)],
                                start=True, stop=True)
            deps = []
            if nt > 0:
                deps.append(cb_ev[nt - 1])
            if e > 0 and nt == 0:
                deps.append(last_cmult_of_expert[e - 1])
            it_cb = _It("pe", f_cb, deps, "cb%d_%d" % (e, nt))
            L["pe"].append(it_cb)

            def f_cbe(v, nt=nt):
                return v.tensor_copy(cbc[:, 512 * nt:512 * (nt + 1)], ps_c[:, :])
            it_ce = _It("ve", f_cbe, [it_cb], "ce%d_%d" % (e, nt))
            L["ve"].append(it_ce)
            cb_ev.append(it_ce)

        for tt in range(4):
            for m in range(16):
                idx = 16 * tt + m
                deps = []
                if idx == 0:
                    deps.append(gp_w1[e])
                if len(relu_items) >= 4:
                    deps.append(relu_items[-4])

                def f_h(t, e=e, tt=tt, m=m):
                    for kc in range(8):
                        mm = t.matmul(
                            ps_h[:, 512 * (m % 4):512 * (m % 4 + 1)],
                            w1b[:, FF * kc + 128 * m:FF * kc + 128 * (m + 1)],
                            xlnT2[:, S * kc + 512 * tt:S * kc + 512 * (tt + 1)],
                            start=(kc == 0), stop=(kc == 7))
                    return mm
                it_h = _It("pe", f_h, deps, "h%d_%d_%d" % (e, tt, m))
                L["pe"].append(it_h)
                last_h[e] = it_h

                rdeps = [it_h]
                if cmult_items:
                    rdeps.append(cmult_items[-1])

                def f_relu(s, e=e, m=m):
                    return s.activation(h_tmp[:, :],
                                        ps_h[:, 512 * (m % 4):512 * (m % 4 + 1)],
                                        AF.Relu,
                                        bias=b1c[:, 16 * e + m:16 * e + m + 1])
                it_r = _It("act", f_relu, rdeps, "r%d_%d_%d" % (e, tt, m))
                L["act"].append(it_r)
                relu_items.append(it_r)

                cdeps = [it_r, cb_ev[tt % 4] if False else cb_ev[3]]
                if tt > 0:
                    cdeps.append(prev_y_last)

                def f_cm(v, tt=tt, m=m):
                    return v.tensor_tensor(h_w[:, 512 * m:512 * (m + 1)],
                                           h_tmp[:, :],
                                           cbc[:, 512 * tt:512 * (tt + 1)],
                                           ALU.mult)
                it_c = _It("ve", f_cm, cdeps, "c%d_%d_%d" % (e, tt, m))
                L["ve"].append(it_c)
                cmult_items.append(it_c)
            last_cm_tt = cmult_items[-1]
            for d in range(8):
                idx = 8 * tt + d
                ydeps = [last_cm_tt]
                if idx == 0:
                    ydeps.append(gp_w2[e])
                if len(yacc_items) >= 2:
                    ydeps.append(yacc_items[-2])

                def f_y(t, e=e, tt=tt, d=d, idx=idx):
                    for kc in range(16):
                        mm = t.matmul(
                            ps_y[idx % 2][:, :],
                            w2b[:, D * kc + 128 * d:D * kc + 128 * (d + 1)],
                            h_w[:, 512 * kc:512 * (kc + 1)],
                            start=(kc == 0),
                            stop=(kc == 15 and e != 0))
                    if e == 0:
                        mm = t.matmul(ps_y[idx % 2][:, :],
                                      b2sb[0:8, 128 * d:128 * (d + 1)],
                                      combT8[0:8, 512 * tt:512 * (tt + 1)],
                                      start=False, stop=True)
                    return mm
                it_y = _It("pe", f_y, ydeps, "y%d_%d_%d" % (e, tt, d))
                L["pe"].append(it_y)
                last_y[e] = it_y

                def f_ya(v, tt=tt, d=d, idx=idx):
                    return v.tensor_tensor(
                        y_acc[:, S * d + 512 * tt:S * d + 512 * (tt + 1)],
                        y_acc[:, S * d + 512 * tt:S * d + 512 * (tt + 1)],
                        ps_y[idx % 2][:, :], ALU.add)
                it_ya = _It("ve", f_ya, [it_y], "ya%d_%d_%d" % (e, tt, d))
                L["ve"].append(it_ya)
                yacc_items.append(it_ya)
            prev_y_last = last_y[e]
        last_cmult_of_expert[e] = cmult_items[-1]
    _run_items(nc, C, sems, L)


def _final(nc, C, sems, y_acc, xspill, out_t, ident, xtile, ostage, ps_o):
    sd, sp, sv, so = sems["sd"], sems["sp"], sems["sv"], sems["so"]
    sd0, sp0, sv0 = C.get("sd"), C.get("sp"), C.get("sv")
    with nc.Block() as blk:
        @blk.sync
        def _(s):
            for tau in range(16):
                if tau >= 2:
                    s.wait_ge(sv, sv0 + tau - 1)
                s.dma_start(xtile[tau % 2][:, :],
                            xspill.ap()[128 * tau:128 * (tau + 1), :]
                            ).then_inc(sd, 16)
                C.add("sd", 16)

        @blk.tensor
        def _(t):
            for tau in range(16):
                if tau >= 2:
                    t.wait_ge(sv, sv0 + tau - 1)
                for d in range(8):
                    mm = t.transpose(ps_o[tau % 2][:, 128 * d:128 * (d + 1)],
                                     y_acc[:, S * d + 128 * tau:S * d + 128 * (tau + 1)],
                                     ident[:, :])
                mm.then_inc(sp, 1); C.add("sp")

        @blk.vector
        def _(vec):
            for tau in range(16):
                vec.wait_ge(sp, sp0 + tau + 1)
                vec.wait_ge(sd, sd0 + 16 * (tau + 1))
                if tau >= 2:
                    vec.wait_ge(so, 16 * (tau - 1))
                vec.tensor_tensor(ostage[tau % 2][:, :], xtile[tau % 2][:, :],
                                  ps_o[tau % 2][:, :], ALU.add).then_inc(sv, 1)
                C.add("sv")

        @blk.gpsimd
        def _(g):
            for tau in range(16):
                g.wait_ge(sv, sv0 + tau + 1)
                g.dma_start(out_t.ap()[128 * tau:128 * (tau + 1), :],
                            ostage[tau % 2][:, :]).then_inc(so, 16)
            g.wait_ge(so, 16 * 16)



def _host_tables():
    pos = np.arange(S, dtype=np.float32)
    i16 = np.arange(HD // 2, dtype=np.float32)
    freq = (np.float32(1.0) /
            np.float32(1e-4) ** (np.float32(2.0) * i16 / np.float32(HD))
            ).astype(np.float32)
    ang = (pos[:, None] * freq[None, :]).astype(np.float32)
    cos32 = np.cos(ang.astype(np.float64)).astype(np.float32)
    sin32 = np.sin(ang.astype(np.float64)).astype(np.float32)
    cos_t = np.ones((128, S), dtype=np.float32)
    sin_t = np.zeros((128, S), dtype=np.float32)
    for p in range(96):
        i = (p % 32) // 2
        sign = -1.0 if p % 2 == 0 else 1.0
        cos_t[p, :] = cos32[:, i]
        sin_t[p, :] = sign * sin32[:, i]
    return cos_t, sin_t


def kernel(L0_attn_x, L0_attn_Wq, L0_attn_Wk, L0_attn_Wv, L0_attn_Wo,
           L0_attn_lambda_q1, L0_attn_lambda_k1, L0_attn_lambda_q2,
           L0_attn_lambda_k2, L0_attn_subln_g, L0_ffn_Wg, L0_ffn_bg,
           L0_ffn_W1, L0_ffn_b1, L0_ffn_W2, L0_ffn_b2):
    f = lambda a: np.ascontiguousarray(np.asarray(a, dtype=np.float32))
    x = f(L0_attn_x)
    b1 = np.asarray(L0_ffn_b1, dtype=np.float32)
    b1cm = np.ascontiguousarray(
        b1.reshape(NEXP, 16, 128).transpose(2, 0, 1).reshape(128, 128))
    small = np.zeros((1, 256), dtype=np.float32)
    small[0, 0:32] = np.asarray(L0_attn_lambda_q1, dtype=np.float32)
    small[0, 32:64] = np.asarray(L0_attn_lambda_k1, dtype=np.float32)
    small[0, 64:96] = np.asarray(L0_attn_lambda_q2, dtype=np.float32)
    small[0, 96:128] = np.asarray(L0_attn_lambda_k2, dtype=np.float32)
    small[0, 128:192] = np.asarray(L0_attn_subln_g, dtype=np.float32)
    small[0, 192:200] = np.asarray(L0_ffn_bg, dtype=np.float32)

    cos_t, sin_t = _host_tables()
    pswap = np.zeros((128, 96), dtype=ml_dtypes.bfloat16)
    for p in range(96):
        pswap[p, p ^ 1] = 1.0
    identb = np.eye(128, dtype=ml_dtypes.bfloat16)
    esel = np.zeros((8, 8 * 128), dtype=ml_dtypes.bfloat16)
    for e in range(8):
        esel[e, 128 * e:128 * (e + 1)] = 1.0

    nc = build()
    shared = dict(wq_in=f(L0_attn_Wq), wk_in=f(L0_attn_Wk), wv_in=f(L0_attn_Wv),
                  wo_in=f(L0_attn_Wo), wg_in=f(L0_ffn_Wg), w1_in=f(L0_ffn_W1),
                  w2_in=f(L0_ffn_W2), b1c_in=b1cm, b2_in=f(L0_ffn_b2),
                  small_in=small, cos_in=cos_t, sin_in=sin_t, pswap_in=pswap,
                  ident_in=identb, esel_in=esel)
    in_maps = [dict(x_in=np.ascontiguousarray(x[i]), **shared) for i in range(8)]
    res = run_bass_kernel_spmd(nc, in_maps, core_ids=list(range(8)))
    return np.stack([r["out"] for r in res.results], axis=0)




# version 13 (final)
# speedup vs baseline: 1.0608x; 1.0608x over previous
import math
import numpy as np
import ml_dtypes
import concourse.bass as bass
import concourse.mybir as mybir
from concourse.bass_utils import run_bass_kernel_spmd
from contextlib import ExitStack

F32 = mybir.dt.float32
BF16 = mybir.dt.bfloat16
AF = mybir.ActivationFunctionType
ALU = mybir.AluOpType
AX = mybir.AxisListType

S = 2048
D = 1024
NH = 16
HD = 32
NSUB = 32
NCHUNK = 11
FF = 2048
NEXP = 8
LAMBDA_INIT = 0.8 - 0.6 * math.exp(-0.3 * 12)
SCALING = HD ** -0.5
EPS = 1e-5


class _It:
    __slots__ = ("eng", "fn", "deps", "cnt", "tag")

    def __init__(self, eng, fn, deps=(), tag=""):
        self.eng, self.fn, self.deps, self.tag = eng, fn, list(deps), tag
        self.cnt = None


def _run_items(nc, C, sems, lists):
    semmap = dict(pe=sems["sp"], act=sems["sa"], ve=sems["sv"], gp=sems["sg"],
                  sy=sems["sd"])
    ctrmap = dict(pe="sp", act="sa", ve="sv", gp="sg", sy="sd")
    step = dict(pe=1, act=1, ve=1, gp=16, sy=16)
    for eng, items in lists.items():
        base = C.get(ctrmap[eng])
        for i, it in enumerate(items):
            it.cnt = base + step[eng] * (i + 1)
        C.add(ctrmap[eng], step[eng] * len(items))

    def emit(engine, eng):
        for it in lists[eng]:
            for dep in it.deps:
                assert dep.cnt is not None, (it.tag, dep.tag)
                engine.wait_ge(semmap[dep.eng], dep.cnt)
            last = it.fn(engine)
            last.then_inc(semmap[it.eng], step[it.eng])

    with nc.Block() as blk:
        if lists.get("pe"):
            blk.tensor(lambda t: emit(t, "pe"))
        if lists.get("act"):
            blk.scalar(lambda s: emit(s, "act"))
        if lists.get("ve"):
            blk.vector(lambda v: emit(v, "ve"))
        if lists.get("gp"):
            blk.gpsimd(lambda g: emit(g, "gp"))
        if lists.get("sy"):
            blk.sync(lambda s: emit(s, "sy"))


def _gv(C, sq, vec, inst, key="sq"):
    inst.then_inc(sq, 1)
    C.add(key)
    vec.wait_ge(sq, C.get(key))


class Ctr:
    def __init__(self):
        self.v = {}

    def add(self, n, by=1):
        self.v[n] = self.v.get(n, 0) + by
        return self.v[n]

    def get(self, n):
        return self.v.get(n, 0)


def build():
    nc = bass.Bass("TRN2", target_bir_lowering=False)

    x_in = nc.dram_tensor("x_in", [S, D], F32, kind="ExternalInput")
    wq_in = nc.dram_tensor("wq_in", [D, D], F32, kind="ExternalInput")
    wk_in = nc.dram_tensor("wk_in", [D, D], F32, kind="ExternalInput")
    wv_in = nc.dram_tensor("wv_in", [D, D], F32, kind="ExternalInput")
    wo_in = nc.dram_tensor("wo_in", [D, D], F32, kind="ExternalInput")
    wg_in = nc.dram_tensor("wg_in", [D, NEXP], F32, kind="ExternalInput")
    w1_in = nc.dram_tensor("w1_in", [NEXP, D, FF], F32, kind="ExternalInput")
    w2_in = nc.dram_tensor("w2_in", [NEXP, FF, D], F32, kind="ExternalInput")
    b1c_in = nc.dram_tensor("b1c_in", [128, 128], F32, kind="ExternalInput")
    b2_in = nc.dram_tensor("b2_in", [NEXP, D], F32, kind="ExternalInput")
    small_in = nc.dram_tensor("small_in", [1, 256], F32, kind="ExternalInput")
    cos_in = nc.dram_tensor("cos_in", [128, S], F32, kind="ExternalInput")
    sin_in = nc.dram_tensor("sin_in", [128, S], F32, kind="ExternalInput")
    pswap_in = nc.dram_tensor("pswap_in", [128, 96], BF16, kind="ExternalInput")
    ident_in = nc.dram_tensor("ident_in", [128, 128], BF16, kind="ExternalInput")
    esel_in = nc.dram_tensor("esel_in", [8, 8 * 128], BF16, kind="ExternalInput")

    out_t = nc.dram_tensor("out", [S, D], F32, kind="ExternalOutput")
    xspill = nc.dram_tensor("xspill", [S, D], F32, kind="Internal")

    C = Ctr()

    def ent(cm):
        return cm.__enter__(), cm

    def exi(*cms):
        for cm in cms:
            cm.__exit__(None, None, None)

    with ExitStack() as top:
        et = top.enter_context
        consts = et(nc.sbuf_tensor("consts", [128, 64], F32, side="left"))
        gcol = et(nc.sbuf_tensor("gcol", [128, 1], F32, side="left"))
        expbias = et(nc.sbuf_tensor("expbias", [128, 1], F32, side="left"))
        onescol = et(nc.sbuf_tensor("onescol", [128, 1], BF16, side="left"))
        onesrow = et(nc.sbuf_tensor("onesrow", [128, 128], BF16, side="left"))
        neglamrow = et(nc.sbuf_tensor("neglamrow", [128, 64], BF16, side="left"))
        ident = et(nc.sbuf_tensor("ident", [128, 128], BF16, side="left"))
        ident32 = et(nc.sbuf_tensor("ident32", [128, 128], F32, side="left"))
        smallv = et(nc.sbuf_tensor("smallv", [1, 256], F32, side="left"))
        b1c = et(nc.sbuf_tensor("b1c", [128, 128], F32, side="left"))
        b2sb = et(nc.sbuf_tensor("b2sb", [8, D], BF16, side="left"))
        esel = et(nc.sbuf_tensor("esel", [8, 8 * 128], BF16, side="left"))
        pswap = et(nc.sbuf_tensor("pswap", [128, 96], BF16, side="left"))
        bgrow = et(nc.sbuf_tensor("bgrow", [1, 8], BF16, side="left"))
        wgsb = et(nc.sbuf_tensor("wgsb", [128, 64], BF16, side="left"))

        sd = et(nc.semaphore("sd"))
        sp = et(nc.semaphore("sp"))
        sv = et(nc.semaphore("sv"))
        sa = et(nc.semaphore("sa"))
        sg = et(nc.semaphore("sg"))
        so = et(nc.semaphore("so"))
        sq = et(nc.semaphore("sq"))
        sq2 = et(nc.semaphore("sq2"))
        sems = dict(sd=sd, sp=sp, sv=sv, sa=sa, sg=sg, so=so, sq=sq, sq2=sq2)

        tabq_c, cm_tqc = ent(nc.sbuf_tensor("tabq_c", [128, S], BF16, side="right"))
        tabq_s, cm_tqs = ent(nc.sbuf_tensor("tabq_s", [128, S], BF16, side="right"))
        tabk_c, cm_tkc = ent(nc.sbuf_tensor("tabk_c", [128, S], BF16, side="right"))
        tabk_s, cm_tks = ent(nc.sbuf_tensor("tabk_s", [128, S], BF16, side="right"))
        xlnT, cm_xlnT = ent(nc.sbuf_tensor("xlnT", [128, 8 * S], BF16, side="right"))
        xln, cm_xln = ent(nc.sbuf_tensor("xln", [128, 16 * D], BF16, side="right"))
        cosf, cm_cosf = ent(nc.sbuf_tensor("cosf", [128, S], F32, side="right"))
        sinf, cm_sinf = ent(nc.sbuf_tensor("sinf", [128, S], F32, side="right"))
        xres, cm_xres = ent(nc.sbuf_tensor("xres", [128, 16 * D], F32, side="right"))
        lnsc, cm_lnsc = ent(nc.sbuf_tensor("lnsc", [128, 8 * 16], F32, side="right"))
        sqd, cm_sqd = ent(nc.sbuf_tensor("sqd", [128, D], BF16, side="right"))
        grow, cm_grow = ent(nc.sbuf_tensor("grow", [1, 64], BF16, side="right"))

        _sc_ph0 = nc.named_scope("ph0"); _sc_ph0.__enter__()
        with ExitStack() as R:
            ps_s = R.enter_context(nc.psum_tensor("ps_s", [128, 128], F32))

            with nc.Block() as blk:
                @blk.sync
                def _(s):
                    for dst, srcap in [(cosf[:, :], cos_in.ap()),
                                       (sinf[:, :], sin_in.ap()),
                                       (smallv[:, :], small_in.ap()),
                                       (b1c[:, :], b1c_in.ap())]:
                        s.dma_start(dst, srcap).then_inc(sd, 16); C.add("sd", 16)
                    s.dma_start(xres[:, :].rearrange("p (a b) -> p a b", a=16),
                                x_in.ap().rearrange("(a p) b -> p a b", p=128)
                                ).then_inc(sd, 16); C.add("sd", 16)
                    for dst, srcap in [(pswap[:, :], pswap_in.ap()),
                                       (ident[:, :], ident_in.ap()),
                                       (esel[:, :], esel_in.ap())]:
                        s.dma_start(dst, srcap).then_inc(sd, 16); C.add("sd", 16)
                    s.wait_ge(sd, C.get("sd"))

                @blk.gpsimd
                def _(g):
                    g.dma_start(b2sb[:, :], b2_in.ap()).then_inc(sg, 16); C.add("sg", 16)
                    g.dma_start(wgsb[:, :].rearrange("p (a b) -> p a b", a=8),
                                wg_in.ap().rearrange("(a p) b -> p a b", p=128)
                                ).then_inc(sg, 16); C.add("sg", 16)
                    g.wait_ge(sg, C.get("sg"))

            with nc.Block() as blk:
                @blk.vector
                def _(vec):
                    _gv(C, sq, vec, vec.tensor_mul(consts[0:1, 4:36], smallv[0:1, 0:32],
                                   smallv[0:1, 32:64]))
                    _gv(C, sq, vec, vec.reduce_sum(consts[0:1, 0:1], consts[0:1, 4:36], AX.X))
                    _gv(C, sq, vec, vec.tensor_mul(consts[0:1, 4:36], smallv[0:1, 64:96],
                                   smallv[0:1, 96:128]))
                    _gv(C, sq, vec, vec.reduce_sum(consts[0:1, 1:2], consts[0:1, 4:36], AX.X))
                    vec.memset(onescol[:, :], 1.0)
                    vec.memset(onesrow[0:1, :], 1.0)
                    vec.memset(onesrow[64:65, :], 1.0)
                    vec.tensor_scalar_mul(tabq_c[:, :], cosf[:, :], float(SCALING))
                    vec.tensor_scalar_mul(tabq_s[:, :], sinf[:, :], float(SCALING))
                    vec.tensor_copy(tabk_c[:, :], cosf[:, :])
                    vec.tensor_copy(tabk_s[:, :], sinf[:, :])
                    vec.tensor_copy(ident32[:, :], ident[:, :])
                    vec.tensor_copy(grow[0:1, 0:64], smallv[0:1, 128:192])
                    vec.tensor_copy(bgrow[0:1, :], smallv[0:1, 192:200])
            with nc.Block() as blk:
                @blk.scalar
                def _(sc):
                    sc.activation(consts[0:1, 2:3], consts[0:1, 0:1], AF.Exp)
                    sc.activation(consts[0:1, 3:4], consts[0:1, 1:2], AF.Exp)
            with nc.Block() as blk:
                @blk.vector
                def _(vec):
                    _gv(C, sq, vec, vec.tensor_sub(consts[0:1, 0:1], consts[0:1, 2:3],
                                   consts[0:1, 3:4]))
                    _gv(C, sq, vec, vec.tensor_scalar(consts[0:1, 0:1], consts[0:1, 0:1],
                                      float(LAMBDA_INIT), None, ALU.add))
                    _gv(C, sq, vec, vec.tensor_scalar_mul(consts[0:1, 1:2], consts[0:1, 0:1], -1.0))
                    vec.tensor_copy(neglamrow[0:1, 0:1], consts[0:1, 1:2])
            with nc.Block() as blk:
                @blk.tensor
                def _(t):
                    t.matmul(ps_s[64:65, 0:64], neglamrow[0:1, 0:1],
                             onesrow[0:1, 0:64], start=True, stop=True)
                    t.matmul(ps_s[0:64, 64:65], grow[0:1, 0:64],
                             onesrow[0:1, 0:1],
                             start=True, stop=True).then_inc(sp, 1)
                    C.add("sp")

                @blk.vector
                def _(vec):
                    vec.wait_ge(sp, C.get("sp"))
                    vec.tensor_copy(neglamrow[64:65, 0:64], ps_s[64:65, 0:64])
                    vec.tensor_scalar_mul(gcol[0:64, 0:1], ps_s[0:64, 64:65],
                                          float(1.0 - LAMBDA_INIT))

            _layernorm(nc, C, sems, xres, xln, lnsc, sqd)

            with nc.Block() as blk:
                @blk.sync
                def _(s):
                    s.dma_start(xspill.ap().rearrange("(a p) b -> p a b", p=128),
                                xres[:, :].rearrange("p (a b) -> p a b", a=16)
                                ).then_inc(sd, 16); C.add("sd", 16)
                    s.wait_ge(sd, C.get("sd"))

        exi(cm_grow, cm_sqd, cm_lnsc, cm_xres, cm_sinf, cm_cosf)

        _sc_ph0.__exit__(None, None, None)
        _sc_phA = nc.named_scope("phA"); _sc_phA.__enter__()
        q_sb, cm_q = ent(nc.sbuf_tensor("q_sb", [128, NCHUNK * S], BF16, side="left"))
        k_sb, cm_k = ent(nc.sbuf_tensor("k_sb", [128, NCHUNK * S], BF16, side="left"))
        v_sb, cm_v = ent(nc.sbuf_tensor("v_sb", [128, 16 * (NH * 65)], BF16, side="left"))

        with ExitStack() as R:
            pst = [R.enter_context(nc.psum_tensor("t1_%d" % _i, [128, 128], BF16))
                   for _i in range(2)]
            _transpose_tok_feat(nc, C, sems, xln, xlnT, ident, pst)
        exi(cm_xln)

        with ExitStack() as R:
            er = R.enter_context
            wbuf = er(nc.sbuf_tensor("wbuf", [128, 8 * D], BF16, side="right"))
            qraw = er(nc.sbuf_tensor("qraw", [128, S], BF16, side="right"))
            t2b = er(nc.sbuf_tensor("t2b", [128, S], BF16, side="right"))
            ps_p = [er(nc.psum_tensor("t2_%d" % _i, [128, 512], F32)) for _i in range(2)]
            ps_w2 = [er(nc.psum_tensor("t3_%d" % _i, [128, 512], F32)) for _i in range(2)]

            _qk_pipeline(nc, C, sems, wq_in, wbuf, xlnT, q_sb, pswap,
                         tabq_c, tabq_s, qraw, t2b, ps_p, ps_w2)
            _qk_pipeline(nc, C, sems, wk_in, wbuf, xlnT, k_sb, pswap,
                         tabk_c, tabk_s, qraw, t2b, ps_p, ps_w2)
            _v_pipeline(nc, C, sems, wv_in, wbuf, xlnT, v_sb, ps_p)

        with ExitStack() as R:
            er = R.enter_context
            sqb = er(nc.sbuf_tensor("sqb", [128, S], BF16, side="right"))
            mxb = er(nc.sbuf_tensor("mxb", [1, 64], F32, side="right"))
            nbb = er(nc.sbuf_tensor("nbb", [1, 1], BF16, side="right"))
            ps_b = er(nc.psum_tensor("t4", [128, S], F32))
            _bound(nc, C, sems, q_sb, k_sb, onescol, onesrow, sqb, mxb, nbb,
                   ps_b, expbias)

        exi(cm_xlnT, cm_tks, cm_tkc, cm_tqs, cm_tqc)

        _sc_phA.__exit__(None, None, None)
        _sc_phB = nc.named_scope("phB"); _sc_phB.__enter__()
        attn2h, cm_attn = ent(nc.sbuf_tensor("attn2h", [128, 8 * S], BF16,
                                             side="right"))
        _attention(nc, C, sems, q_sb, k_sb, v_sb, attn2h, expbias, onesrow,
                   neglamrow, onescol, gcol, ident)
        exi(cm_v, cm_k, cm_q)

        _sc_phB.__exit__(None, None, None)
        _sc_phC = nc.named_scope("phC"); _sc_phC.__enter__()
        xlnT2 = et(nc.sbuf_tensor("xlnT2", [128, 8 * S], BF16, side="left"))
        combT8 = et(nc.sbuf_tensor("combT8", [8, S], BF16, side="left"))
        y_acc = et(nc.sbuf_tensor("y_acc", [128, 8 * S], BF16, side="left"))

        with ExitStack() as R:
            er = R.enter_context
            wo_sb = er(nc.sbuf_tensor("wo_sb", [128, 8 * D], BF16, side="right"))
            wstage = er(nc.sbuf_tensor("wstage", [128, 512], F32, side="right"))
            xrt = er(nc.sbuf_tensor("xrt", [128, 4 * D], F32, side="right"))
            xl2t = er(nc.sbuf_tensor("xl2t", [128, D], BF16, side="right"))
            lns2 = er(nc.sbuf_tensor("lns2", [128, 8], F32, side="right"))
            gsc = er(nc.sbuf_tensor("gsc", [128, 64], F32, side="right"))
            comb_bf = er(nc.sbuf_tensor("comb_bf", [128, 8], BF16, side="right"))
            ps_w = [er(nc.psum_tensor("t5_%d" % _i, [128, 512], F32)) for _i in range(2)]
            ps_t = [er(nc.psum_tensor("t6_%d" % _i, [128, 128], F32)) for _i in range(2)]
            ps_tb = [er(nc.psum_tensor("t7_%d" % _i, [128, 128], BF16)) for _i in range(2)]
            ps_g = er(nc.psum_tensor("t8", [128, 8], F32))
            ps_8 = er(nc.psum_tensor("t9", [8, 128], F32))

            with nc.Block() as blk:
                @blk.gpsimd
                def _(g):
                    g.dma_start(wo_sb[:, :].rearrange("p (a b) -> p a b", a=8),
                                wo_in.ap().rearrange("(a p) b -> p a b", p=128)
                                ).then_inc(sg, 16); C.add("sg", 16)
                    g.wait_ge(sg, C.get("sg"))

            _wo_ln2_gate(nc, C, sems, wo_sb, attn2h, wstage, xrt, xspill,
                         xl2t, lns2, xlnT2, gsc, comb_bf, combT8, wgsb, bgrow,
                         onesrow, ident, ident32,
                         ps_w, ps_t, ps_tb, ps_g, ps_8)

        exi(cm_attn)

        _sc_phC.__exit__(None, None, None)
        _sc_phE = nc.named_scope("phE"); _sc_phE.__enter__()
        with ExitStack() as R:
            er = R.enter_context
            w1b = er(nc.sbuf_tensor("w1b", [128, 8 * FF], BF16, side="right"))
            w2b = er(nc.sbuf_tensor("w2b", [128, 16 * D], BF16, side="right"))
            h_w = er(nc.sbuf_tensor("h_w", [128, 16 * 512], BF16, side="right"))
            h_tmp = er(nc.sbuf_tensor("h_tmp", [128, 512], BF16, side="right"))
            cbc = er(nc.sbuf_tensor("cbc", [128, 2 * S], BF16, side="right"))
            ps_h = er(nc.psum_tensor("t10", [128, 4 * 512], F32))
            ps_y = [er(nc.psum_tensor("t11_%d" % _i, [128, 512], F32)) for _i in range(2)]
            ps_c = er(nc.psum_tensor("t12", [128, 512], F32))
            ps_c2 = er(nc.psum_tensor("t12b", [128, 512], F32))

            _ffn(nc, C, sems, w1_in, w2_in, xlnT2, w1b, w2b, h_w, h_tmp, cbc,
                 b1c, b2sb, combT8, esel, y_acc, ps_h, ps_y, ps_c, ps_c2=ps_c2)

        _sc_phE.__exit__(None, None, None)
        _sc_phF = nc.named_scope("phF"); _sc_phF.__enter__()
        with ExitStack() as R:
            er = R.enter_context
            xtile = [er(nc.sbuf_tensor("t13_%d" % _i, [128, D], F32, side="right"))
                     for _i in range(2)]
            ostage = [er(nc.sbuf_tensor("t14_%d" % _i, [128, D], F32, side="right"))
                      for _i in range(2)]
            ps_o = [er(nc.psum_tensor("t15_%d" % _i, [128, D], BF16)) for _i in range(2)]
            _final(nc, C, sems, y_acc, xspill, out_t, ident, xtile, ostage, ps_o)

    _sc_phF.__exit__(None, None, None)
    return nc



def _layernorm(nc, C, sems, xres, xln, lnsc, sqd):
    sv, sa = sems["sv"], sems["sa"]
    sa0 = C.get("sa")
    with nc.Block() as blk:
        @blk.vector
        def _(vec):
            vec.memset(lnsc[:, :], 0.0).then_inc(sv, 1)
            C.add("sv")
    with nc.Block() as blk:
        @blk.scalar
        def _(sc):
            for tau in range(16):
                sc.activation(sqd[:, :], xres[:, D * tau:D * (tau + 1)],
                              AF.Square,
                              accum_out=lnsc[:, 8 * tau + 1:8 * tau + 2]
                              ).then_inc(sa, 1)
                C.add("sa")

        @blk.vector
        def _(vec):
            sq = sems["sq"]
            for tau in range(16):
                o = 8 * tau
                vec.wait_ge(sa, sa0 + tau + 1)
                _gv(C, sq, vec, vec.reduce_sum(lnsc[:, o:o + 1],
                               xres[:, D * tau:D * (tau + 1)], AX.X))
                _gv(C, sq, vec, vec.tensor_scalar_mul(lnsc[:, o + 2:o + 3],
                               lnsc[:, o:o + 1], 1.0 / D))
                _gv(C, sq, vec, vec.tensor_mul(lnsc[:, o + 3:o + 4],
                               lnsc[:, o + 2:o + 3], lnsc[:, o + 2:o + 3]))
                _gv(C, sq, vec, vec.tensor_scalar(lnsc[:, o + 4:o + 5],
                               lnsc[:, o + 1:o + 2],
                               1.0 / D, float(EPS), ALU.mult, ALU.add))
                vec.tensor_sub(lnsc[:, o + 4:o + 5], lnsc[:, o + 4:o + 5],
                               lnsc[:, o + 3:o + 4]).then_inc(sv, 1)
                C.add("sv")
    sa1 = C.get("sa")
    sq = sems["sq"]
    with nc.Block() as blk:
        @blk.scalar
        def _(sc):
            sq2 = sems["sq2"]
            for tau in range(16):
                o = 8 * tau
                _gv(C, sq2, sc, sc.activation(lnsc[:, o + 5:o + 6],
                               lnsc[:, o + 4:o + 5], AF.Ln), key="sq2")
                sc.activation(lnsc[:, o + 6:o + 7], lnsc[:, o + 5:o + 6],
                              AF.Exp, scale=-0.5).then_inc(sa, 1)
                C.add("sa")

        @blk.vector
        def _(vec):
            for tau in range(16):
                o = 8 * tau
                vec.wait_ge(sa, sa1 + tau + 1)
                vec.tensor_scalar(xln[:, D * tau:D * (tau + 1)],
                                  xres[:, D * tau:D * (tau + 1)],
                                  lnsc[:, o + 2:o + 3], lnsc[:, o + 6:o + 7],
                                  ALU.subtract, ALU.mult)


def _transpose_tok_feat(nc, C, sems, xln, xlnT, ident, pst):
    sp, sv = sems["sp"], sems["sv"]
    sp0, sv0 = C.get("sp"), C.get("sv")
    with nc.Block() as blk:
        @blk.tensor
        def _(t):
            i = 0
            for tau in range(16):
                for fc in range(8):
                    if i >= 2:
                        t.wait_ge(sv, sv0 + i - 1)
                    t.transpose(pst[i % 2][:, :],
                                xln[:, D * tau + 128 * fc:D * tau + 128 * (fc + 1)],
                                ident[:, :]).then_inc(sp, 1)
                    C.add("sp")
                    i += 1

        @blk.vector
        def _(vec):
            i = 0
            for tau in range(16):
                for fc in range(8):
                    vec.wait_ge(sp, sp0 + i + 1)
                    vec.tensor_copy(
                        xlnT[:, S * fc + 128 * tau:S * fc + 128 * (tau + 1)],
                        pst[i % 2][:, :]).then_inc(sv, 1)
                    C.add("sv")
                    i += 1


def _qk_pipeline(nc, C, sems, w_in, wbuf, xlnT, dst, pswap, ctab, stab,
                 qraw, t2b, ps_p, ps_w2):
    sd, sp, sv, sg = sems["sd"], sems["sp"], sems["sv"], sems["sg"]
    with nc.Block() as blk:
        @blk.gpsimd
        def _(g):
            g.dma_start(wbuf[:, :].rearrange("p (a b) -> p a b", a=8),
                        w_in.ap().rearrange("(a p) b -> p a b", p=128)
                        ).then_inc(sg, 16); C.add("sg", 16)
            g.wait_ge(sg, C.get("sg"))

    sp0, sv0 = C.get("sp"), C.get("sv")
    pe_plan = []
    ve_plan = []
    cp, cv = sp0, sv0
    for c in range(NCHUNK):
        m = 96 if c < 10 else 64
        for nt in range(4):
            idx = 4 * c + nt
            wv = (sv0 + _qk_vcount(idx - 2) + 1) if idx >= 2 else None
            pe_plan.append(("proj", c, nt, idx % 2, m, wv))
            cp += 1
            ve_plan.append(("evac", c, nt, idx % 2, m, cp))
            cv += 1
        for nt in range(4):
            sidx = 4 * c + nt
            wv = sv0 + _qk_vcount_chunk_evacs(c)
            wv2 = (sv0 + _qk_vcount_swap(sidx - 2) + 1) if sidx >= 2 else None
            pe_plan.append(("swap", c, nt, nt % 2, m, wv, wv2))
            cp += 1
            ve_plan.append(("ropet2", c, nt, nt % 2, m, cp))
            cv += 1
        ve_plan.append(("ropefin", c, 0, 0, m, cp))
        cv += 1

    with nc.Block() as blk:
        @blk.tensor
        def _(t):
            for item in pe_plan:
                if item[0] == "proj":
                    _, c, nt, slot, m, wv = item
                    if wv is not None:
                        t.wait_ge(sv, wv)
                    for kc in range(8):
                        mm = t.matmul(
                            ps_p[slot][0:m, :],
                            wbuf[:, D * kc + 96 * c:D * kc + 96 * c + m],
                            xlnT[:, S * kc + 512 * nt:S * kc + 512 * (nt + 1)],
                            start=(kc == 0), stop=(kc == 7))
                    mm.then_inc(sp, 1); C.add("sp")
                else:
                    _, c, nt, slot, m, wv, wv2 = item
                    t.wait_ge(sv, wv)
                    if wv2 is not None:
                        t.wait_ge(sv, wv2)
                    t.matmul(ps_w2[slot][0:m, :], pswap[0:m, 0:m],
                             qraw[0:m, 512 * nt:512 * (nt + 1)],
                             start=True, stop=True).then_inc(sp, 1)
                    C.add("sp")

        @blk.vector
        def _(vec):
            for item in ve_plan:
                kind, c, nt, slot, m = item[0], item[1], item[2], item[3], item[4]
                if kind == "evac":
                    vec.wait_ge(sp, item[5])
                    vec.tensor_copy(qraw[0:m, 512 * nt:512 * (nt + 1)],
                                    ps_p[slot][0:m, :]).then_inc(sv, 1)
                    C.add("sv")
                elif kind == "ropet2":
                    vec.wait_ge(sp, item[5])
                    vec.tensor_tensor(t2b[0:m, 512 * nt:512 * (nt + 1)],
                                      ps_w2[slot][0:m, :],
                                      stab[0:m, 512 * nt:512 * (nt + 1)],
                                      ALU.mult).then_inc(sv, 1)
                    C.add("sv")
                else:
                    vec.tensor_tensor(qraw[0:m, :], qraw[0:m, :], ctab[0:m, :],
                                      ALU.mult)
                    vec.tensor_tensor(dst[0:m, S * c:S * (c + 1)],
                                      qraw[0:m, :], t2b[0:m, :],
                                      ALU.add).then_inc(sv, 1)
                    C.add("sv")

    assert C.get("sp") == cp and C.get("sv") == cv, (C.get("sp"), cp, C.get("sv"), cv)


def _qk_vcount(proj_idx):
    c, nt = divmod(proj_idx, 4)
    return 9 * c + nt


def _qk_vcount_chunk_evacs(c):
    return 9 * c + 4


def _qk_vcount_swap(sidx):
    c, nt = divmod(sidx, 4)
    return 9 * c + 4 + nt


def _v_pipeline(nc, C, sems, wv_in, wbuf, xlnT, v_sb, ps_p):
    sd, sp, sv, sg = sems["sd"], sems["sp"], sems["sv"], sems["sg"]
    with nc.Block() as blk:
        @blk.gpsimd
        def _(g):
            g.dma_start(wbuf[:, :].rearrange("p (a b) -> p a b", a=8),
                        wv_in.ap().rearrange("(a p) b -> p a b", p=128)
                        ).then_inc(sg, 16); C.add("sg", 16)
            g.wait_ge(sg, C.get("sg"))

        @blk.vector
        def _(vec):
            vec.memset(
                v_sb[:, :].rearrange("p (a b c) -> p a b c", a=16, b=NH)[:, :, :, 64:65],
                1.0)

    sp0, sv0 = C.get("sp"), C.get("sv")
    with nc.Block() as blk:
        @blk.tensor
        def _(t):
            for j in range(16):
                for nt in range(2):
                    idx = 2 * j + nt
                    if idx >= 2:
                        t.wait_ge(sv, sv0 + idx - 1)
                    for kc in range(8):
                        mm = t.matmul(
                            ps_p[idx % 2][:, :],
                            xlnT[:, S * kc + 128 * j:S * kc + 128 * (j + 1)],
                            wbuf[:, D * kc + 512 * nt:D * kc + 512 * (nt + 1)],
                            start=(kc == 0), stop=(kc == 7))
                    mm.then_inc(sp, 1); C.add("sp")

        @blk.vector
        def _(vec):
            for j in range(16):
                for nt in range(2):
                    idx = 2 * j + nt
                    vec.wait_ge(sp, sp0 + idx + 1)
                    dst = v_sb[:, :].rearrange("p (a b c) -> p a b c",
                                               a=16, b=NH)[:, j, 8 * nt:8 * (nt + 1), 0:64]
                    vec.tensor_copy(dst,
                                    ps_p[idx % 2][:, :].rearrange(
                                        "p (h d) -> p h d", h=8)
                                    ).then_inc(sv, 1)
                    C.add("sv")


def _bound(nc, C, sems, q_sb, k_sb, onescol, onesrow, sqb, mxb, nbb, ps_b,
           expbias):
    sp, sv, sa = sems["sp"], sems["sv"], sems["sa"]
    for wi, src in enumerate([q_sb, k_sb]):
        sa0 = C.get("sa")
        sp0 = C.get("sp")
        sv0 = C.get("sv")
        with nc.Block() as blk:
            @blk.scalar
            def _(sc):
                for c in range(NCHUNK):
                    m = 96 if c < 10 else 64
                    if c >= 1:
                        sc.wait_ge(sp, sp0 + c)
                    sc.activation(sqb[0:m, :], src[0:m, S * c:S * (c + 1)],
                                  AF.Square).then_inc(sa, 1)
                    C.add("sa")

            @blk.tensor
            def _(t):
                for c in range(NCHUNK):
                    m = 96 if c < 10 else 64
                    t.wait_ge(sa, sa0 + c + 1)
                    if c >= 1:
                        t.wait_ge(sv, sv0 + c)
                    for nt in range(4):
                        mm = t.matmul(ps_b[0:1, 512 * nt:512 * (nt + 1)],
                                      onescol[0:m, 0:1],
                                      sqb[0:m, 512 * nt:512 * (nt + 1)],
                                      start=True, stop=True)
                    mm.then_inc(sp, 1); C.add("sp")

            @blk.vector
            def _(vec):
                for c in range(NCHUNK):
                    vec.wait_ge(sp, sp0 + c + 1)
                    vec.reduce_max(mxb[0:1, 32 * wi + c:32 * wi + c + 1],
                                   ps_b[0:1, :], AX.X).then_inc(sv, 1)
                    C.add("sv")

    with nc.Block() as blk:
        @blk.vector
        def _(vec):
            sq = sems["sq"]
            _gv(C, sq, vec, vec.reduce_max(mxb[0:1, 20:21], mxb[0:1, 0:11], AX.X))
            _gv(C, sq, vec, vec.reduce_max(mxb[0:1, 21:22], mxb[0:1, 32:43], AX.X))
            vec.tensor_mul(mxb[0:1, 22:23], mxb[0:1, 20:21], mxb[0:1, 21:22])
    with nc.Block() as blk:
        @blk.scalar
        def _(sc):
            sq2 = sems["sq2"]
            _gv(C, sq2, sc, sc.activation(mxb[0:1, 24:25], mxb[0:1, 22:23], AF.Ln),
                key="sq2")
            sc.activation(mxb[0:1, 23:24], mxb[0:1, 24:25], AF.Exp, scale=0.5)
    with nc.Block() as blk:
        @blk.vector
        def _(vec):
            vec.tensor_scalar_mul(nbb[0:1, 0:1], mxb[0:1, 23:24], -1.0)
    with nc.Block() as blk:
        @blk.tensor
        def _(t):
            t.matmul(ps_b[:, 0:1], onesrow[0:1, 0:128], nbb[0:1, 0:1],
                     start=True, stop=True).then_inc(sp, 1)
            C.add("sp")

        @blk.vector
        def _(vec):
            vec.wait_ge(sp, C.get("sp"))
            vec.tensor_copy(expbias[:, 0:1], ps_b[:, 0:1])


def _attention(nc, C, sems, q_sb, k_sb, v_sb, attn2h, expbias, onesrow,
               neglamrow, onescol, gcol, ident):
    sq = sems["sq"]
    with ExitStack() as R:
        er = R.enter_context
        probs = [er(nc.sbuf_tensor("at_pr%d" % _i, [128, 3 * 512], BF16, side="right"))
                 for _i in range(2)]
        u_sb = er(nc.sbuf_tensor("at_u", [128, 6 * 512], F32, side="right"))
        zrf = er(nc.sbuf_tensor("at_zrf", [128, 1024], F32, side="right"))
        zr = er(nc.sbuf_tensor("at_zr", [128, 1024], BF16, side="right"))
        bca = er(nc.sbuf_tensor("at_bca", [64, 512], BF16, side="right"))
        bcb = er(nc.sbuf_tensor("at_bcb", [64, 512], BF16, side="right"))
        tmp1 = er(nc.sbuf_tensor("at_t1", [64, 512], BF16, side="right"))
        tmp2 = er(nc.sbuf_tensor("at_t2", [64, 512], BF16, side="right"))
        comb_o = er(nc.sbuf_tensor("at_co", [64, 512], BF16, side="right"))
        sqs = er(nc.sbuf_tensor("at_sq", [64, 512], BF16, side="right"))
        rmsr = er(nc.sbuf_tensor("at_rr", [1, 1024], F32, side="right"))
        rms_bf = er(nc.sbuf_tensor("at_rb", [1, 512], BF16, side="right"))
        mvd = er(nc.sbuf_tensor("at_mv", [64, 512], BF16, side="right"))
        ps_sc = er(nc.psum_tensor("at_psc", [128, 3 * 512], F32))
        ps_pv = [er(nc.psum_tensor("at_ppv%d" % _i, [65, 512], F32))
                 for _i in range(3)]
        ps_bc = er(nc.psum_tensor("at_pbc", [128, 512], F32))
        ps_bc2 = er(nc.psum_tensor("at_pbc2", [128, 512], F32))

        L = dict(pe=[], act=[], ve=[], gp=[])
        pend_pe, pend_act = [], []
        free_slots = [0, 1, 2]
        slot_evac = {0: None, 1: None, 2: None}
        free_uslots = [0, 1, 2, 3, 4, 5]
        uslot_free_item = {u: None for u in range(6)}
        hold = {}
        last_exp = None
        last_move = None
        prev_pe2 = None
        prev_v4 = None
        prev_v2 = None

        def vslice(j, h):
            return v_sb[:, :].rearrange("p (a b) -> p a b", a=16)[:, j,
                                                                 65 * h:65 * (h + 1)]

        for qt in range(4):
            for c in range(NCHUNK):
                subs = list(range(3 * c, min(3 * c + 3, NSUB)))
                nsp = len(subs)
                slots = [free_slots.pop(0) for _ in subs]
                uslots = [free_uslots.pop(0) for _ in subs]
                for s_, ul in zip(subs, uslots):
                    hold[s_] = ul

                pv_items = []
                pend_pv = None
                for j in range(16):
                    deps = []
                    if last_exp is not None:
                        deps.append(last_exp)
                    if j == 0:
                        for sl in slots:
                            if slot_evac[sl] is not None:
                                deps.append(slot_evac[sl])

                    def f_scores(t, qt=qt, c=c, j=j, subs=subs):
                        for i_s, s_ in enumerate(subs):
                            spp = s_ - 3 * c
                            mm = t.matmul(
                                ps_sc[:, 512 * i_s:512 * (i_s + 1)],
                                k_sb[32 * spp:32 * (spp + 1),
                                     S * c + 128 * j:S * c + 128 * (j + 1)],
                                q_sb[32 * spp:32 * (spp + 1),
                                     S * c + 512 * qt:S * c + 512 * (qt + 1)],
                                start=True, stop=True,
                                tile_position=(32 * spp, 0))
                        return mm
                    it_sc = _It("pe", f_scores, deps, "sc%d_%d_%d" % (qt, c, j))
                    L["pe"].append(it_sc)

                    def f_exp(s, j=j, nsp=nsp):
                        return s.activation(probs[j % 2][:, 0:512 * nsp],
                                            ps_sc[:, 0:512 * nsp], AF.Exp,
                                            bias=expbias[:, 0:1])
                    it_exp = _It("act", f_exp, [it_sc], "ex%d_%d_%d" % (qt, c, j))
                    L["act"].append(it_exp)
                    last_exp = it_exp

                    def f_pv(t, qt=qt, c=c, j=j, subs=subs, slots=slots):
                        for i_s, s_ in enumerate(subs):
                            mm = t.matmul(
                                ps_pv[slots[i_s]][0:65, :],
                                vslice(j, s_ // 2),
                                probs[j % 2][:, 512 * i_s:512 * (i_s + 1)],
                                start=(j == 0), stop=(j == 15))
                        return mm
                    it_pv = _It("pe", f_pv, [it_exp], "pv%d_%d_%d" % (qt, c, j))
                    pv_items.append(it_pv)
                    if pend_pv is not None:
                        L["pe"].append(pend_pv)
                    pend_pv = it_pv
                    while pend_pe and pend_pe[0][0] <= j:
                        L["pe"].append(pend_pe.pop(0)[1])
                    while pend_act and pend_act[0][0] <= j:
                        L["act"].append(pend_act.pop(0)[1])



                if pend_pv is not None:
                    L["pe"].append(pend_pv)
                    pend_pv = None
                ev_deps = [pv_items[-1]]
                for ul in uslots:
                    if uslot_free_item[ul] is not None:
                        ev_deps.append(uslot_free_item[ul])

                def f_evac(v, subs=subs, slots=slots, uslots=uslots):
                    for i_s in range(len(subs)):
                        last = v.tensor_copy(u_sb[0:65, 512 * uslots[i_s]:
                                                  512 * uslots[i_s] + 512],
                                             ps_pv[slots[i_s]][0:65, :])
                    return last
                it_ev = _It("ve", f_evac, ev_deps, "ev%d_%d" % (qt, c))
                L["ve"].append(it_ev)
                for sl in slots:
                    slot_evac[sl] = it_ev
                free_slots.extend(slots)

                kpair = 0
                for s_ in subs:
                    if s_ % 2 != 1:
                        continue
                    sA, sB = hold[s_ - 1], hold[s_]
                    h = s_ // 2
                    del hold[s_ - 1], hold[s_]

                    def f_act1(s, sA=sA, sB=sB):
                        s.activation(zrf[64:65, 0:512],
                                     u_sb[64:65, 512 * sA:512 * sA + 512], AF.Ln)
                        i2 = s.activation(zrf[64:65, 512:1024],
                                          u_sb[64:65, 512 * sB:512 * sB + 512],
                                          AF.Ln)
                        _gv(C, sems["sq2"], s, i2, key="sq2")
                        return s.activation(zr[64:65, 0:1024], zrf[64:65, 0:1024],
                                            AF.Exp, scale=-1.0)
                    d1 = [it_ev] + ([prev_pe2] if prev_pe2 is not None else [])
                    it_a1 = _It("act", f_act1, d1, "a1_%d_%d" % (qt, h))

                    def f_bca(t):
                        return t.matmul(ps_bc[0:64, :], onesrow[64:65, 0:64],
                                        zr[64:65, 0:512], start=True, stop=True)
                    d_p1 = [it_a1] + ([prev_v4] if prev_v4 is not None else [])
                    it_p1 = _It("pe", f_bca, d_p1, "p1_%d_%d" % (qt, h))

                    def f_bcb(t):
                        return t.matmul(ps_bc2[0:64, :], neglamrow[64:65, 0:64],
                                        zr[64:65, 512:1024], start=True, stop=True)
                    d_p2 = [it_a1] + ([prev_v2] if prev_v2 is not None else [])
                    it_p2 = _It("pe", f_bcb, d_p2, "p2_%d_%d" % (qt, h))
                    prev_pe2 = it_p2

                    def f_comb(v, sA=sA, sB=sB):
                        v.tensor_tensor(tmp1[0:64, :],
                                        u_sb[0:64, 512 * sA:512 * sA + 512],
                                        ps_bc[0:64, :], ALU.mult)
                        v.tensor_tensor(tmp2[0:64, :],
                                        u_sb[0:64, 512 * sB:512 * sB + 512],
                                        ps_bc2[0:64, :], ALU.mult)
                        i4 = v.tensor_tensor(comb_o[0:64, :], tmp1[0:64, :],
                                             tmp2[0:64, :], ALU.add)
                        _gv(C, sq, v, i4)
                        return v.tensor_tensor(sqs[0:64, :], comb_o[0:64, :],
                                               comb_o[0:64, :], ALU.mult)
                    it_v2 = _It("ve", f_comb, [it_p1, it_p2], "v2_%d_%d" % (qt, h))
                    prev_v2 = it_v2

                    def f_sel(t):
                        return t.matmul(ps_bc[0:1, :], onescol[0:64, 0:1],
                                        sqs[0:64, :], start=True, stop=True)
                    it_p3 = _It("pe", f_sel, [it_v2], "p3_%d_%d" % (qt, h))

                    def f_rp(v):
                        return v.tensor_scalar(rmsr[0:1, 0:512], ps_bc[0:1, :],
                                               1.0 / 64, float(EPS), ALU.mult,
                                               ALU.add)
                    it_v3 = _It("ve", f_rp, [it_p3], "v3_%d_%d" % (qt, h))

                    def f_act2(s):
                        i1 = s.activation(rmsr[0:1, 512:1024], rmsr[0:1, 0:512],
                                          AF.Ln)
                        _gv(C, sems["sq2"], s, i1, key="sq2")
                        return s.activation(rms_bf[0:1, :], rmsr[0:1, 512:1024],
                                            AF.Exp, scale=-0.5)
                    it_a2 = _It("act", f_act2, [it_v3], "a2_%d_%d" % (qt, h))

                    def f_rbc(t):
                        return t.matmul(ps_bc[0:64, :], onesrow[0:1, 0:64],
                                        rms_bf[0:1, :], start=True, stop=True)
                    it_p4 = _It("pe", f_rbc, [it_a2], "p4_%d_%d" % (qt, h))

                    def f_apply(v, h=h, qt=qt):
                        v.tensor_scalar(mvd[0:64, :] if False else tmp1[0:64, :],
                                        ps_bc[0:64, :], gcol[0:64, 0:1], None,
                                        ALU.mult)
                        if h % 2 == 0:
                            return v.tensor_tensor(
                                attn2h[0:64, S * (h // 2) + 512 * qt:
                                       S * (h // 2) + 512 * (qt + 1)],
                                comb_o[0:64, :], tmp1[0:64, :], ALU.mult)
                        return v.tensor_tensor(mvd[0:64, :], comb_o[0:64, :],
                                               tmp1[0:64, :], ALU.mult)
                    d4 = [it_p4] + ([last_move] if (h % 2 == 1 and
                                                    last_move is not None) else [])
                    it_v4 = _It("ve", f_apply, d4, "v4_%d_%d" % (qt, h))

                    L["ve"].extend([it_v2, it_v3, it_v4])
                    b_ = 6 * kpair
                    pend_act.extend([(b_ + 0, it_a1), (b_ + 3, it_a2)])
                    pend_pe.extend([(b_ + 1, it_p1), (b_ + 1, it_p2),
                                    (b_ + 2, it_p3), (b_ + 4, it_p4)])
                    kpair += 1
                    prev_v4 = it_v4
                    for ul in (sA, sB):
                        uslot_free_item[ul] = it_v2
                        free_uslots.append(ul)

                    if h % 2 == 1:
                        def f_move(g, h=h, qt=qt):
                            return g.dma_start(
                                attn2h[64:128, S * (h // 2) + 512 * qt:
                                       S * (h // 2) + 512 * (qt + 1)],
                                mvd[0:64, :])
                        it_gp = _It("gp", f_move, [it_v4], "mv%d_%d" % (qt, h))
                        L["gp"].append(it_gp)
                        last_move = it_gp
        L["pe"].extend(it for _, it in pend_pe)
        L["act"].extend(it for _, it in pend_act)
        _run_items(nc, C, sems, L)
        with nc.Block() as blk:
            @blk.gpsimd
            def _(g):
                g.wait_ge(sems["sg"], C.get("sg"))


def _wo_ln2_gate(nc, C, sems, wo_sb, attn2h, wstage, xrt, xspill, xl2t, lns2,
                 xlnT2, gsc, comb_bf, combT8, wgsb, bgrow, onesrow, ident,
                 ident32, ps_w, ps_t, ps_tb, ps_g, ps_8):
    sd, sp, sv, sa = sems["sd"], sems["sp"], sems["sv"], sems["sa"]

    for qt in range(4):
        with nc.Block() as blk:
            @blk.sync
            def _(s):
                s.wait_ge(sd, C.get("sd"))
                for qi in range(4):
                    tau = 4 * qt + qi
                    s.dma_start(xrt[:, D * qi:D * (qi + 1)],
                                xspill.ap()[128 * tau:128 * (tau + 1), :]
                                ).then_inc(sd, 16); C.add("sd", 16)
                s.wait_ge(sd, C.get("sd"))

        sp0, sv0 = C.get("sp"), C.get("sv")
        pe_plan, ve_plan = [], []
        cp, cv = sp0, sv0
        for dout in range(8):
            pe_plan.append(("wo", dout))
            cp += 1
            ve_plan.append(("evac", dout, cp))
            cv += 1
            for qi in range(4):
                pe_plan.append(("tr", dout, qi))
                cp += 1
                ve_plan.append(("add", dout, qi, cp))
                cv += 1
        def pe_idx(kind, d, qi=0):
            if kind == "wo":
                return 5 * d + 1
            return 5 * d + 2 + qi

        with nc.Block() as blk:
            @blk.tensor
            def _(t):
                for item in pe_plan:
                    if item[0] == "wo":
                        d = item[1]
                        if d >= 2:
                            t.wait_ge(sv, sv0 + 5 * (d - 2) + 1)
                        for hp in range(8):
                            mm = t.matmul(
                                ps_w[d % 2][:, :],
                                wo_sb[:, D * hp + 128 * d:D * hp + 128 * (d + 1)],
                                attn2h[:, S * hp + 512 * qt:S * hp + 512 * (qt + 1)],
                                start=(hp == 0), stop=(hp == 7))
                        mm.then_inc(sp, 1); C.add("sp")
                    else:
                        d, qi = item[1], item[2]
                        t.wait_ge(sv, sv0 + 5 * d + 1)
                        if qi >= 2:
                            t.wait_ge(sv, sv0 + 5 * d + 2 + (qi - 2))
                        elif d >= 1:
                            t.wait_ge(sv, sv0 + 5 * (d - 1) + 2 + qi + 2)
                        t.transpose(ps_t[qi % 2][:, :],
                                    wstage[:, 128 * qi:128 * (qi + 1)],
                                    ident32[:, :]).then_inc(sp, 1)
                        C.add("sp")

            @blk.vector
            def _(vec):
                for item in ve_plan:
                    if item[0] == "evac":
                        d = item[1]
                        vec.wait_ge(sp, sp0 + pe_idx("wo", d))
                        if d >= 1:
                            vec.wait_ge(sp, sp0 + pe_idx("tr", d - 1, 3))
                        vec.tensor_copy(wstage[:, :], ps_w[d % 2][:, :]
                                        ).then_inc(sv, 1)
                        C.add("sv")
                    else:
                        d, qi = item[1], item[2]
                        vec.wait_ge(sp, sp0 + pe_idx("tr", d, qi))
                        vec.tensor_tensor(
                            xrt[:, D * qi + 128 * d:D * qi + 128 * (d + 1)],
                            xrt[:, D * qi + 128 * d:D * qi + 128 * (d + 1)],
                            ps_t[qi % 2][:, :], ALU.add).then_inc(sv, 1)
                        C.add("sv")

        sq = sems["sq"]
        sq2 = sems["sq2"]
        Lt = dict(pe=[], act=[], ve=[], sy=[])
        prev = dict(v_nm=None, a_le=None, p_tr_last=None, v_te=[None] * 8,
                    v_g1=None, v_ce=None)
        for qi in range(4):
            tau = 4 * qt + qi
            xt = xrt[:, D * qi:D * (qi + 1)]

            def f_ms(v):
                return v.memset(lns2[:, 1:2], 0.0)
            dms = [x for x in (prev["v_nm"], prev["a_le"]) if x is not None]
            i_ms = _It("ve", f_ms, dms, "cms%d" % tau)
            Lt["ve"].append(i_ms)

            def f_sq(s, xt=xt):
                return s.activation(xl2t[:, :], xt, AF.Square,
                                    accum_out=lns2[:, 1:2])
            dsq = [i_ms] + ([prev["p_tr_last"]] if prev["p_tr_last"] else [])
            i_sq = _It("act", f_sq, dsq, "csq%d" % tau)
            Lt["act"].append(i_sq)

            def f_st(v, xt=xt):
                _gv(C, sq, v, v.reduce_sum(lns2[:, 0:1], xt, AX.X))
                _gv(C, sq, v, v.tensor_scalar_mul(lns2[:, 2:3], lns2[:, 0:1],
                                                  1.0 / D))
                _gv(C, sq, v, v.tensor_mul(lns2[:, 3:4], lns2[:, 2:3],
                                           lns2[:, 2:3]))
                _gv(C, sq, v, v.tensor_scalar(lns2[:, 4:5], lns2[:, 1:2],
                                              1.0 / D, float(EPS), ALU.mult,
                                              ALU.add))
                return v.tensor_sub(lns2[:, 4:5], lns2[:, 4:5], lns2[:, 3:4])
            i_st = _It("ve", f_st, [i_sq], "cst%d" % tau)
            Lt["ve"].append(i_st)

            def f_le(s):
                _gv(C, sq2, s, s.activation(lns2[:, 5:6], lns2[:, 4:5], AF.Ln),
                    key="sq2")
                return s.activation(lns2[:, 6:7], lns2[:, 5:6], AF.Exp,
                                    scale=-0.5)
            i_le = _It("act", f_le, [i_st], "cle%d" % tau)
            Lt["act"].append(i_le)
            prev["a_le"] = i_le

            def f_nm(v, xt=xt):
                return v.tensor_scalar(xl2t[:, :], xt, lns2[:, 2:3],
                                       lns2[:, 6:7], ALU.subtract, ALU.mult)
            i_nm = _It("ve", f_nm, [i_le], "cnm%d" % tau)
            Lt["ve"].append(i_nm)
            prev["v_nm"] = i_nm

            def f_sp(s, xt=xt, tau=tau):
                return s.dma_start(xspill.ap()[128 * tau:128 * (tau + 1), :], xt)
            Lt["sy"].append(_It("sy", f_sp, [i_nm], "csp%d" % tau))

            for fc in range(8):
                def f_tr(t, fc=fc):
                    return t.transpose(ps_tb[fc % 2][:, :],
                                       xl2t[:, 128 * fc:128 * (fc + 1)],
                                       ident[:, :])
                dtr = [i_nm]
                if prev["v_te"][fc - 2] is not None:
                    dtr.append(prev["v_te"][fc - 2])
                i_tr = _It("pe", f_tr, dtr, "ctr%d_%d" % (tau, fc))
                Lt["pe"].append(i_tr)

                def f_te(v, fc=fc, tau=tau):
                    return v.tensor_copy(
                        xlnT2[:, S * fc + 128 * tau:S * fc + 128 * (tau + 1)],
                        ps_tb[fc % 2][:, :])
                i_te = _It("ve", f_te, [i_tr], "cte%d_%d" % (tau, fc))
                Lt["ve"].append(i_te)
                prev["v_te"][fc] = i_te
            prev["p_tr_last"] = i_tr

            def f_g(t, tau=tau):
                for kc in range(8):
                    t.matmul(ps_g[:, :],
                             xlnT2[:, S * kc + 128 * tau:S * kc + 128 * (tau + 1)],
                             wgsb[:, 8 * kc:8 * (kc + 1)],
                             start=(kc == 0), stop=False)
                return t.matmul(ps_g[:, :], onesrow[0:1, 0:128], bgrow[0:1, :],
                                start=False, stop=True)
            dg = [prev["v_te"][7]] + ([prev["v_g1"]] if prev["v_g1"] else [])
            i_g = _It("pe", f_g, dg, "cg%d" % tau)
            Lt["pe"].append(i_g)

            def f_g1(v):
                _gv(C, sq, v, v.reduce_max(gsc[:, 0:1], ps_g[:, :], AX.X))
                return v.tensor_single_scalar(gsc[:, 8:16], ps_g[:, :],
                                              gsc[:, 0:1], ALU.subtract)
            i_g1 = _It("ve", f_g1, [i_g], "cg1%d" % tau)
            Lt["ve"].append(i_g1)
            prev["v_g1"] = i_g1

            def f_ex(s):
                return s.activation(gsc[:, 16:24], gsc[:, 8:16], AF.Exp)
            i_ex = _It("act", f_ex, [i_g1], "cex%d" % tau)
            Lt["act"].append(i_ex)

            def f_g2(v):
                e = gsc[:, 16:24]
                m1, m2 = gsc[:, 1:2], gsc[:, 2:3]
                mask1, gm, mask2 = gsc[:, 24:32], gsc[:, 32:40], gsc[:, 40:48]
                den, dinv = gsc[:, 3:4], gsc[:, 4:5]
                _gv(C, sq, v, v.reduce_max(m1, e, AX.X))
                _gv(C, sq, v, v.tensor_single_scalar(mask1, e, m1, ALU.is_ge))
                _gv(C, sq, v, v.scalar_tensor_tensor(gm, mask1, -1e30, e,
                                                     ALU.mult, ALU.add))
                _gv(C, sq, v, v.reduce_max(m2, gm, AX.X))
                _gv(C, sq, v, v.tensor_single_scalar(mask2, e, m2, ALU.is_ge))
                _gv(C, sq, v, v.tensor_add(den, m1, m2))
                _gv(C, sq, v, v.reciprocal(dinv, den))
                _gv(C, sq, v, v.scalar_tensor_tensor(gsc[:, 48:56], e, dinv,
                                                     mask2, ALU.mult, ALU.mult))
                return v.tensor_copy(comb_bf[:, :], gsc[:, 48:56])
            i_g2 = _It("ve", f_g2, [i_ex], "cg2%d" % tau)
            Lt["ve"].append(i_g2)

            def f_p8(t):
                return t.matmul(ps_8[0:8, :], comb_bf[:, 0:8], ident[:, :],
                                start=True, stop=True)
            dp8 = [i_g2] + ([prev["v_ce"]] if prev["v_ce"] else [])
            i_p8 = _It("pe", f_p8, dp8, "cp8%d" % tau)
            Lt["pe"].append(i_p8)

            def f_ce(v, tau=tau):
                return v.tensor_copy(combT8[0:8, 128 * tau:128 * (tau + 1)],
                                     ps_8[0:8, :])
            i_ce = _It("ve", f_ce, [i_p8], "cce%d" % tau)
            Lt["ve"].append(i_ce)
            prev["v_ce"] = i_ce
        _run_items(nc, C, sems, Lt)


def _ln2_tile(nc, C, sems, xt, xl2t, lns2):
    sv, sa = sems["sv"], sems["sa"]
    with nc.Block() as blk:
        @blk.vector
        def _(vec):
            vec.memset(lns2[:, 1:2], 0.0)
    with nc.Block() as blk:
        @blk.scalar
        def _(sc):
            sc.activation(xl2t[:, :], xt, AF.Square,
                          accum_out=lns2[:, 1:2]).then_inc(sa, 1)
            C.add("sa")

        @blk.vector
        def _(vec):
            sq = sems["sq"]
            vec.wait_ge(sa, C.get("sa"))
            _gv(C, sq, vec, vec.reduce_sum(lns2[:, 0:1], xt, AX.X))
            _gv(C, sq, vec, vec.tensor_scalar_mul(lns2[:, 2:3], lns2[:, 0:1], 1.0 / D))
            _gv(C, sq, vec, vec.tensor_mul(lns2[:, 3:4], lns2[:, 2:3], lns2[:, 2:3]))
            _gv(C, sq, vec, vec.tensor_scalar(lns2[:, 4:5], lns2[:, 1:2], 1.0 / D,
                              float(EPS), ALU.mult, ALU.add))
            vec.tensor_sub(lns2[:, 4:5], lns2[:, 4:5], lns2[:, 3:4]
                           ).then_inc(sv, 1)
            C.add("sv")
    with nc.Block() as blk:
        @blk.scalar
        def _(sc):
            sq2 = sems["sq2"]
            _gv(C, sq2, sc, sc.activation(lns2[:, 5:6], lns2[:, 4:5], AF.Ln),
                key="sq2")
            sc.activation(lns2[:, 6:7], lns2[:, 5:6], AF.Exp,
                          scale=-0.5).then_inc(sa, 1)
            C.add("sa")

        @blk.vector
        def _(vec):
            vec.wait_ge(sa, C.get("sa"))
            vec.tensor_scalar(xl2t[:, :], xt, lns2[:, 2:3], lns2[:, 6:7],
                              ALU.subtract, ALU.mult)


def _ffn(nc, C, sems, w1_in, w2_in, xlnT2, w1b, w2b, h_w, h_tmp, cbc,
         b1c, b2sb, combT8, esel, y_acc, ps_h, ps_y, ps_c, ps_c2=None):
    with nc.Block() as blk:
        @blk.vector
        def _(vec):
            vec.memset(y_acc[:, :], 0.0)

    L = dict(pe=[], act=[], ve=[], gp=[])
    last_h = {}
    last_y = {}
    relu_items = []
    cmult_items = []
    yacc_items = []
    last_cmult_of_expert = {}
    gp_w1 = {}
    gp_w2 = {}

    for e in range(NEXP):
        def f_w1(g, e=e):
            return g.dma_start(w1b[:, :].rearrange("p (a b) -> p a b", a=8),
                               w1_in.ap()[e].rearrange("(a p) b -> p a b", p=128))
        gp_w1[e] = _It("gp", f_w1,
                       [last_h[e - 1]] if e > 0 else [], "w1_%d" % e)
        L["gp"].append(gp_w1[e])

        def f_w2(g, e=e):
            return g.dma_start(w2b[:, :].rearrange("p (a b) -> p a b", a=16),
                               w2_in.ap()[e].rearrange("(a p) b -> p a b", p=128))
        gp_w2[e] = _It("gp", f_w2,
                       [last_y[e - 1]] if e > 0 else [], "w2_%d" % e)
        L["gp"].append(gp_w2[e])

        cb_ev = []
        half = e % 2
        for nt in range(4):
            pc = ps_c if nt % 2 == 0 else ps_c2

            def f_cb(t, e=e, nt=nt, pc=pc):
                return t.matmul(pc[:, :], esel[0:8, 128 * e:128 * (e + 1)],
                                combT8[0:8, 512 * nt:512 * (nt + 1)],
                                start=True, stop=True)
            deps = []
            if nt > 1:
                deps.append(cb_ev[nt - 2])
            if e > 1 and nt == 0:
                deps.append(last_cmult_of_expert[e - 2])
            it_cb = _It("pe", f_cb, deps, "cb%d_%d" % (e, nt))
            L["pe"].append(it_cb)

            def f_cbe(v, nt=nt, pc=pc, half=half):
                return v.tensor_copy(cbc[:, S * half + 512 * nt:
                                          S * half + 512 * (nt + 1)], pc[:, :])
            it_ce = _It("ve", f_cbe, [it_cb], "ce%d_%d" % (e, nt))
            L["ve"].append(it_ce)
            cb_ev.append(it_ce)

        for tt in range(4):
            for m in range(16):
                idx = 16 * tt + m
                deps = []
                if idx == 0:
                    deps.append(gp_w1[e])
                if len(relu_items) >= 4:
                    deps.append(relu_items[-4])

                def f_h(t, e=e, tt=tt, m=m):
                    for kc in range(8):
                        mm = t.matmul(
                            ps_h[:, 512 * (m % 4):512 * (m % 4 + 1)],
                            w1b[:, FF * kc + 128 * m:FF * kc + 128 * (m + 1)],
                            xlnT2[:, S * kc + 512 * tt:S * kc + 512 * (tt + 1)],
                            start=(kc == 0), stop=(kc == 7))
                    return mm
                it_h = _It("pe", f_h, deps, "h%d_%d_%d" % (e, tt, m))
                L["pe"].append(it_h)
                last_h[e] = it_h

                rdeps = [it_h]
                if cmult_items:
                    rdeps.append(cmult_items[-1])

                def f_relu(s, e=e, m=m):
                    return s.activation(h_tmp[:, :],
                                        ps_h[:, 512 * (m % 4):512 * (m % 4 + 1)],
                                        AF.Relu,
                                        bias=b1c[:, 16 * e + m:16 * e + m + 1])
                it_r = _It("act", f_relu, rdeps, "r%d_%d_%d" % (e, tt, m))
                L["act"].append(it_r)
                relu_items.append(it_r)

                cdeps = [it_r, cb_ev[tt % 4] if False else cb_ev[3]]
                if tt > 0:
                    cdeps.append(prev_y_last)

                def f_cm(v, tt=tt, m=m, half=e % 2):
                    return v.tensor_tensor(h_w[:, 512 * m:512 * (m + 1)],
                                           h_tmp[:, :],
                                           cbc[:, S * half + 512 * tt:
                                               S * half + 512 * (tt + 1)],
                                           ALU.mult)
                it_c = _It("ve", f_cm, cdeps, "c%d_%d_%d" % (e, tt, m))
                L["ve"].append(it_c)
                cmult_items.append(it_c)
            last_cm_tt = cmult_items[-1]
            for d in range(8):
                idx = 8 * tt + d
                ydeps = [last_cm_tt]
                if idx == 0:
                    ydeps.append(gp_w2[e])
                if len(yacc_items) >= 2:
                    ydeps.append(yacc_items[-2])

                def f_y(t, e=e, tt=tt, d=d, idx=idx):
                    for kc in range(16):
                        mm = t.matmul(
                            ps_y[idx % 2][:, :],
                            w2b[:, D * kc + 128 * d:D * kc + 128 * (d + 1)],
                            h_w[:, 512 * kc:512 * (kc + 1)],
                            start=(kc == 0),
                            stop=(kc == 15 and e != 0))
                    if e == 0:
                        mm = t.matmul(ps_y[idx % 2][:, :],
                                      b2sb[0:8, 128 * d:128 * (d + 1)],
                                      combT8[0:8, 512 * tt:512 * (tt + 1)],
                                      start=False, stop=True)
                    return mm
                it_y = _It("pe", f_y, ydeps, "y%d_%d_%d" % (e, tt, d))
                L["pe"].append(it_y)
                last_y[e] = it_y

                def f_ya(v, tt=tt, d=d, idx=idx):
                    return v.tensor_tensor(
                        y_acc[:, S * d + 512 * tt:S * d + 512 * (tt + 1)],
                        y_acc[:, S * d + 512 * tt:S * d + 512 * (tt + 1)],
                        ps_y[idx % 2][:, :], ALU.add)
                it_ya = _It("ve", f_ya, [it_y], "ya%d_%d_%d" % (e, tt, d))
                L["ve"].append(it_ya)
                yacc_items.append(it_ya)
            prev_y_last = last_y[e]
        last_cmult_of_expert[e] = cmult_items[-1]
    _run_items(nc, C, sems, L)


def _final(nc, C, sems, y_acc, xspill, out_t, ident, xtile, ostage, ps_o):
    sd, sp, sv, so = sems["sd"], sems["sp"], sems["sv"], sems["so"]
    sd0, sp0, sv0 = C.get("sd"), C.get("sp"), C.get("sv")
    with nc.Block() as blk:
        @blk.sync
        def _(s):
            for tau in range(16):
                if tau >= 2:
                    s.wait_ge(sv, sv0 + tau - 1)
                s.dma_start(xtile[tau % 2][:, :],
                            xspill.ap()[128 * tau:128 * (tau + 1), :]
                            ).then_inc(sd, 16)
                C.add("sd", 16)

        @blk.tensor
        def _(t):
            for tau in range(16):
                if tau >= 2:
                    t.wait_ge(sv, sv0 + tau - 1)
                for d in range(8):
                    mm = t.transpose(ps_o[tau % 2][:, 128 * d:128 * (d + 1)],
                                     y_acc[:, S * d + 128 * tau:S * d + 128 * (tau + 1)],
                                     ident[:, :])
                mm.then_inc(sp, 1); C.add("sp")

        @blk.vector
        def _(vec):
            for tau in range(16):
                vec.wait_ge(sp, sp0 + tau + 1)
                vec.wait_ge(sd, sd0 + 16 * (tau + 1))
                if tau >= 2:
                    vec.wait_ge(so, 16 * (tau - 1))
                vec.tensor_tensor(ostage[tau % 2][:, :], xtile[tau % 2][:, :],
                                  ps_o[tau % 2][:, :], ALU.add).then_inc(sv, 1)
                C.add("sv")

        @blk.gpsimd
        def _(g):
            for tau in range(16):
                g.wait_ge(sv, sv0 + tau + 1)
                g.dma_start(out_t.ap()[128 * tau:128 * (tau + 1), :],
                            ostage[tau % 2][:, :]).then_inc(so, 16)
            g.wait_ge(so, 16 * 16)



def _host_tables():
    pos = np.arange(S, dtype=np.float32)
    i16 = np.arange(HD // 2, dtype=np.float32)
    freq = (np.float32(1.0) /
            np.float32(1e-4) ** (np.float32(2.0) * i16 / np.float32(HD))
            ).astype(np.float32)
    ang = (pos[:, None] * freq[None, :]).astype(np.float32)
    cos32 = np.cos(ang.astype(np.float64)).astype(np.float32)
    sin32 = np.sin(ang.astype(np.float64)).astype(np.float32)
    cos_t = np.ones((128, S), dtype=np.float32)
    sin_t = np.zeros((128, S), dtype=np.float32)
    for p in range(96):
        i = (p % 32) // 2
        sign = -1.0 if p % 2 == 0 else 1.0
        cos_t[p, :] = cos32[:, i]
        sin_t[p, :] = sign * sin32[:, i]
    return cos_t, sin_t


def kernel(L0_attn_x, L0_attn_Wq, L0_attn_Wk, L0_attn_Wv, L0_attn_Wo,
           L0_attn_lambda_q1, L0_attn_lambda_k1, L0_attn_lambda_q2,
           L0_attn_lambda_k2, L0_attn_subln_g, L0_ffn_Wg, L0_ffn_bg,
           L0_ffn_W1, L0_ffn_b1, L0_ffn_W2, L0_ffn_b2):
    f = lambda a: np.ascontiguousarray(np.asarray(a, dtype=np.float32))
    x = f(L0_attn_x)
    b1 = np.asarray(L0_ffn_b1, dtype=np.float32)
    b1cm = np.ascontiguousarray(
        b1.reshape(NEXP, 16, 128).transpose(2, 0, 1).reshape(128, 128))
    small = np.zeros((1, 256), dtype=np.float32)
    small[0, 0:32] = np.asarray(L0_attn_lambda_q1, dtype=np.float32)
    small[0, 32:64] = np.asarray(L0_attn_lambda_k1, dtype=np.float32)
    small[0, 64:96] = np.asarray(L0_attn_lambda_q2, dtype=np.float32)
    small[0, 96:128] = np.asarray(L0_attn_lambda_k2, dtype=np.float32)
    small[0, 128:192] = np.asarray(L0_attn_subln_g, dtype=np.float32)
    small[0, 192:200] = np.asarray(L0_ffn_bg, dtype=np.float32)

    cos_t, sin_t = _host_tables()
    pswap = np.zeros((128, 96), dtype=ml_dtypes.bfloat16)
    for p in range(96):
        pswap[p, p ^ 1] = 1.0
    identb = np.eye(128, dtype=ml_dtypes.bfloat16)
    esel = np.zeros((8, 8 * 128), dtype=ml_dtypes.bfloat16)
    for e in range(8):
        esel[e, 128 * e:128 * (e + 1)] = 1.0

    nc = build()
    shared = dict(wq_in=f(L0_attn_Wq), wk_in=f(L0_attn_Wk), wv_in=f(L0_attn_Wv),
                  wo_in=f(L0_attn_Wo), wg_in=f(L0_ffn_Wg), w1_in=f(L0_ffn_W1),
                  w2_in=f(L0_ffn_W2), b1c_in=b1cm, b2_in=f(L0_ffn_b2),
                  small_in=small, cos_in=cos_t, sin_in=sin_t, pswap_in=pswap,
                  ident_in=identb, esel_in=esel)
    in_maps = [dict(x_in=np.ascontiguousarray(x[i]), **shared) for i in range(8)]
    res = run_bass_kernel_spmd(nc, in_maps, core_ids=list(range(8)))
    return np.stack([r["out"] for r in res.results], axis=0)


